# version 14 (final)
# baseline (speedup 1.0000x reference)
import numpy as np
import contextlib
import concourse.bass as bass
import concourse.mybir as mybir
from concourse.bass_utils import run_bass_kernel_spmd

F32 = mybir.dt.float32
BF16 = mybir.dt.bfloat16
AF = mybir.ActivationFunctionType
ALU = mybir.AluOpType
AX = mybir.AxisListType

ENGINES = ("tensor", "vector", "scalar", "gpsimd", "sync")

D = 1024
FF = 4096
NTOK = 4096
SG = 2048
NCHUNK = 16
NE = 10
NPIECE = 4
NH = 16
DH = 64
DIN = 2048
NST = 16
EPS = 1e-5
GN_EPS = 1e-5

class Prog:
    def __init__(self, nc, es):
        self.nc = nc
        self.es = es
        self.items = {e: [] for e in ENGINES}
        self.prog_sem = {}
        self.prog_cnt = {e: 0 for e in ENGINES}
        for e in ENGINES:
            self.prog_sem[e] = es.enter_context(nc.semaphore(f"prog_{e}"))
        self.dma_sem = {}
        self.dma_cnt = {}
        self.last_write = {}
        self.readers = {}
        self.waited = {e: {} for e in ENGINES}
        self.sem_handles = {}
        for e in ENGINES:
            self.sem_handles["prog_" + e] = self.prog_sem[e]
        self.scheduled_total = {("prog_" + e): 0 for e in ENGINES}
        self.final_waits = []

    DMA_WAYS = 4

    def _stream_sems(self, stream):
        if stream not in self.dma_sem:
            hs = []
            for j in range(self.DMA_WAYS):
                name = f"dma_{stream}_{j}"
                h = self.es.enter_context(self.nc.semaphore(name))
                hs.append(h)
                self.sem_handles[name] = h
                self.scheduled_total[name] = 0
            self.dma_sem[stream] = hs
            self.dma_cnt[stream] = 0
        return self.dma_sem[stream]

    def _gather_waits(self, engine, reads, writes):
        evs = []
        for r in reads:
            lw = self.last_write.get(r)
            if lw:
                evs.append(lw)
        for w in writes:
            lw = self.last_write.get(w)
            if lw:
                evs.append(lw)
            evs.extend(self.readers.get(w, ()))
        best = {}
        for semname, val in evs:
            if best.get(semname, -1) < val:
                best[semname] = val
        waits = []
        for semname, val in best.items():
            if engine == "tensor" and semname == "prog_tensor":
                continue
            if self.waited[engine].get(semname, -1) >= val:
                continue
            assert val <= self.scheduled_total[semname], (
                f"wait {semname}>={val} exceeds scheduled {self.scheduled_total[semname]}")
            waits.append((semname, val))
            self.waited[engine][semname] = val
        return waits

    def _finish(self, engine, completion, reads, writes):
        for w in writes:
            self.last_write[w] = completion
            self.readers[w] = []
        for r in reads:
            self.readers.setdefault(r, []).append(completion)

    def emit(self, engine, fn, reads=(), writes=(), tag="", n_inc=1):
        reads = list(reads); writes = list(writes)
        waits = self._gather_waits(engine, reads, writes)
        semname = "prog_" + engine
        self.prog_cnt[engine] += 1
        cnt = self.prog_cnt[engine]
        self.scheduled_total[semname] = cnt
        self.items[engine].append(("op", waits, fn, self.prog_sem[engine], 1, tag))
        completion = (semname, cnt)
        self._finish(engine, completion, reads, writes)
        return completion

    def dma(self, qengine, stream, out, in_, reads=(), writes=(), tag="", **kw):
        sems = self._stream_sems(stream)
        n = self.dma_cnt[stream]
        j = n % self.DMA_WAYS
        semname = f"dma_{stream}_{j}"
        sem = sems[j]
        reads = list(reads); writes = list(writes)
        waits = self._gather_waits(qengine, reads, writes)
        prev = 16 * (n // self.DMA_WAYS)
        if prev > 0 and self.waited[qengine].get(semname, -1) < prev:
            waits.append((semname, prev))
            self.waited[qengine][semname] = prev
        self.dma_cnt[stream] = n + 1
        val = 16 * (n // self.DMA_WAYS + 1)
        self.scheduled_total[semname] = val

        def fn(eng, out=out, in_=in_, kw=kw):
            return eng.dma_start(out=out, in_=in_, **kw)

        self.items[qengine].append(("op", waits, fn, sem, 16, tag))
        completion = (semname, val)
        self._finish(qengine, completion, reads, writes)
        return completion

    def barrier_wait(self, engine, events):
        waits = []
        for semname, val in events:
            if self.waited[engine].get(semname, -1) >= val:
                continue
            assert val <= self.scheduled_total[semname]
            waits.append((semname, val))
            self.waited[engine][semname] = val
        if waits:
            self.items[engine].append(("waits", waits, None, None, 0, "barrier"))

    def wait_all_current(self, engine):
        evs = [(s, v) for s, v in self.scheduled_total.items() if v > 0]
        self.barrier_wait(engine, evs)

    def build_block(self):
        nc = self.nc
        with nc.Block() as block:
            for ename in ENGINES:
                if not self.items[ename]:
                    continue
                def make(ename=ename):
                    def body(eng):
                        for kind, waits, fn, sem, inc, tag in self.items[ename]:
                            for semname, val in waits:
                                eng.wait_ge(self.sem_handles[semname], val)
                            if fn is not None:
                                inst = fn(eng)
                                if sem is not None:
                                    if inc == 1:
                                        inst.then_inc(sem, 1)
                                    else:
                                        inst.then_inc(sem, inc)
                    return body
                getattr(block, ename)(make())
        self.items = {e: [] for e in ENGINES}


_NAME_CTR = [0]


def _uname(name):
    _NAME_CTR[0] += 1
    return f"{name}__{_NAME_CTR[0]}"


def sbuf(nc, es, name, shape, dtype):
    return es.enter_context(nc.sbuf_tensor(_uname(name), shape, dtype))


class ConstCols:
    def __init__(self, p, es, values):
        self.t = sbuf(p.nc, es, "constcols", [128, max(1, len(values))], F32)
        self.idx = {}
        for j, (k, v) in enumerate(values.items()):
            self.idx[k] = j
            p.emit("vector", lambda e, j=j, v=v: e.memset(self.t[:, j:j+1], v),
                   writes=[("cc", k)], tag=f"const_{k}")

    def col(self, k):
        return self.t[:, self.idx[k]:self.idx[k]+1]

    def key(self, k):
        return ("cc", k)


def psum(nc, es, name, shape, dtype=F32):
    return es.enter_context(nc.psum_tensor(_uname(name), shape, dtype))





def pack_lhsT_blob(W):
    A = np.asarray(W)
    arr = A.reshape(1, 8, 128, 8, 128).transpose(0, 4, 1, 3, 2)
    return np.ascontiguousarray(arr.reshape(128, 8192))


def pack_rhs_blob(WT):
    A = np.asarray(WT)
    nk = A.shape[0] // 128
    return np.ascontiguousarray(A.reshape(nk, 128, A.shape[1]).transpose(1, 0, 2))


def make_gamma_tiles():
    g = np.asarray([1.0 - 2.0 ** (-5.0 - i) for i in range(NH)], dtype=np.float64)
    idx = np.arange(128, dtype=np.float64)
    diff = idx[None, :] - idx[:, None]
    out = np.zeros((NH, 2, 128, 128), np.float32)
    for h in range(NH):
        full = g[h] ** diff
        out[h, 0] = full.astype(np.float32)
        out[h, 1] = np.where(diff >= 0, full, 0.0).astype(np.float32)
    return out


class LnCtx:
    def __init__(self, p, es, pref, stream=False, tp=None, tpkey=None):
        nc = p.nc
        self.stream = stream
        nxb = 2 if stream else 16
        self.xb = sbuf(nc, es, pref + "_xb", [128, nxb, 1024], F32)
        self.xlnT = sbuf(nc, es, pref + "_xlnT", [128, 8, 2048], BF16)
        self.xln_tm = sbuf(nc, es, pref + "_xlntm", [128, 2, 1024], BF16)
        if tp is None:
            tp = psum(nc, es, pref + "_tp", [128, 2, 1024], BF16)
            tpkey = pref + "_tp"
        self.tp = tp
        self.tpkey = tpkey
        self.sq = sbuf(nc, es, pref + "_sq", [128, 16], F32)
        self.sm = sbuf(nc, es, pref + "_sm", [128, 16], F32)
        self.mn = sbuf(nc, es, pref + "_mn", [128, 16], F32)
        self.m2t = sbuf(nc, es, pref + "_m2t", [128, 16], F32)
        self.var = sbuf(nc, es, pref + "_var", [128, 16], F32)
        self.lv = sbuf(nc, es, pref + "_lv", [128, 16], F32)
        self.rstd = sbuf(nc, es, pref + "_rstd", [128, 16], F32)
        self.junk = sbuf(nc, es, pref + "_junk", [128, 1024], F32)


def emit_ln_sg(p, lx, cc, xsrc, sg, xsrc_key, keep_x=True):
    t0 = sg * SG
    stream = getattr(lx, "stream", False)
    for i in range(NCHUNK):
        ii = (i % 2) if stream else i
        rows = slice(t0 + i * 128, t0 + (i + 1) * 128)
        xi = lx.xb[:, ii, :]
        p.dma("sync", "lnx", xi, xsrc[rows, :],
              reads=[(xsrc_key, sg, i)], writes=[("xb", ii)], tag="lnx")
        p.emit("vector", lambda v, a=xi, i=i, ii=ii: v.scalar_tensor_tensor(
            lx.junk[:], a, 1.0, a, ALU.mult, ALU.mult, accum_out=lx.sq[:, i:i+1]),
            reads=[("xb", ii)], writes=[("sq",), ("junk",)], tag="sumsq")
        p.emit("vector", lambda v, a=xi, i=i, ii=ii: v.reduce_sum(
            lx.sm[:, i:i+1], a, axis=AX.X),
            reads=[("xb", ii)], writes=[("sm",)], tag="sum")
    p.emit("scalar", lambda s: s.mul(lx.mn[:], lx.sm[:], 1.0 / D),
           reads=[("sm",)], writes=[("mn",)])
    p.emit("scalar", lambda s: s.activation(lx.m2t[:], lx.mn[:], AF.Square),
           reads=[("mn",)], writes=[("m2t",)])
    p.emit("vector", lambda v: v.scalar_tensor_tensor(
        lx.var[:], lx.sq[:], 1.0 / D, lx.m2t[:], ALU.mult, ALU.subtract),
        reads=[("sq",), ("m2t",)], writes=[("var",)])
    p.emit("scalar", lambda s: s.activation(lx.lv[:], lx.var[:], AF.Ln,
                                            bias=cc.col("eps")),
           reads=[("var",), cc.key("eps")], writes=[("lv",)])
    p.emit("scalar", lambda s: s.activation(lx.rstd[:], lx.lv[:], AF.Exp, scale=-0.5),
           reads=[("lv",)], writes=[("rstd",)])
    for i in range(NCHUNK):
        b = i % 2
        ii = (i % 2) if stream else i
        if stream:
            rows = slice(t0 + i * 128, t0 + (i + 1) * 128)
            p.dma("sync", "lnx", lx.xb[:, ii, :], xsrc[rows, :],
                  reads=[(xsrc_key, sg, i)], writes=[("xb", ii)], tag="lnx2")
        xt = lx.xln_tm[:, b, :]
        p.emit("vector", lambda v, i=i, ii=ii, xt=xt: v.tensor_scalar(
            xt, lx.xb[:, ii, :], lx.mn[:, i:i+1], lx.rstd[:, i:i+1],
            ALU.subtract, ALU.mult),
            reads=[("xb", ii), ("mn",), ("rstd",)], writes=[("xlntm", b)], tag="ln")
        for kc in range(8):
            j = kc % 2
            p.emit("tensor", lambda t, kc=kc, j=j, xt=xt: t.transpose(
                lx.tp[:, j, 0:128], xt[:, kc*128:(kc+1)*128], p.idbf),
                reads=[("xlntm", b), ("idbf",)], writes=[(lx.tpkey, j)],
                tag="lnT")
            p.emit("scalar", lambda s, kc=kc, j=j, i=i: s.copy(
                lx.xlnT[:, kc, i*128:(i+1)*128], lx.tp[:, j, 0:128]),
                reads=[(lx.tpkey, j)], writes=[("xlnT",)], tag="lnTe")


def emit_attn_a_proj(p, lx, ax, sg):
    nc = p.nc
    t0 = sg * SG
    for wi, (wsb, stash) in enumerate(((ax.wq, ax.qT), (ax.wk, ax.kT))):
        wkeys = [("wh", wi, j) for j in range(4)]
        for m in range(8):
            for g in range(4):
                pp = (m * 4 + g) % 2
                def mk(t, wsb=wsb, m=m, g=g, pp=pp):
                    last = None
                    for kc in range(8):
                        last = t.matmul(
                            ax.ps_p[:, pp, :],
                            wsb[:, m*1024 + kc*128: m*1024 + (kc+1)*128],
                            lx.xlnT[:, kc, g*512:(g+1)*512],
                            start=(kc == 0), stop=(kc == 7))
                    return last
                p.emit("tensor", mk, reads=wkeys + [("xlnT",)],
                       writes=[("psp", pp)], tag=f"qk{wi}.{m}.{g}")
                eb = (m * 4 + g) % 2
                p.emit("scalar", lambda s, pp=pp, eb=eb: s.copy(
                    ax.stg[:, eb, :], ax.ps_p[:, pp, :]),
                    reads=[("psp", pp)], writes=[("stg", eb)], tag="pevac")
                p.dma("gpsimd", "stq", stash[m*128:(m+1)*128,
                                                 t0 + g*512: t0 + (g+1)*512],
                      ax.stg[:, eb, :],
                      reads=[("stg", eb)], writes=[(f"qk{wi}", sg, m, g)], tag="stash")
    for i in range(NCHUNK):
        for n in range(2):
            pp = (i * 2 + n) % 2
            def mkv(t, i=i, n=n, pp=pp):
                last = None
                for kc in range(8):
                    last = t.matmul(
                        ax.ps_p[:, pp, :],
                        lx.xlnT[:, kc, i*128:(i+1)*128],
                        ax.wv[:, kc, n*512:(n+1)*512],
                        start=(kc == 0), stop=(kc == 7))
                return last
            p.emit("tensor", mkv, reads=[("wh", 2, 0), ("xlnT",)],
                   writes=[("psp", pp)], tag=f"v{i}.{n}")
            eb = (i * 2 + n) % 2
            p.emit("scalar", lambda s, pp=pp, eb=eb: s.copy(
                ax.stg[:, eb, :], ax.ps_p[:, pp, :]),
                reads=[("psp", pp)], writes=[("stg", eb)], tag="pevac")
            p.dma("gpsimd", "stq", ax.vS[t0 + i*128: t0 + (i+1)*128,
                                          n*512:(n+1)*512],
                  ax.stg[:, eb, :],
                  reads=[("stg", eb)], writes=[("vst", sg, i, n)], tag="stash")


class AttnProjCtx:
    def __init__(self, p, es):
        nc = p.nc
        self.wq = sbuf(nc, es, "aw_q", [128, 8192], BF16)[:]
        self.wk = sbuf(nc, es, "aw_k", [128, 8192], BF16)[:]
        self.wv = sbuf(nc, es, "aw_v", [128, 8, 1024], BF16)
        self.stg = sbuf(nc, es, "a_stg", [128, 2, 512], BF16)
        self.ps_p = psum(nc, es, "a_psp", [128, 2, 512], F32)


class RetCtx:
    def __init__(self, p, es):
        nc = p.nc
        self.qTb = sbuf(nc, es, "r_qTb", [128, 8, 2048], BF16)
        self.kTb = sbuf(nc, es, "r_kTb", [128, 8, 2048], BF16)
        self.vb = sbuf(nc, es, "r_vb", [128, 16, 1024], BF16)
        self.gma = sbuf(nc, es, "r_gma", [128, 16, 2, 128], BF16)
        self.att = sbuf(nc, es, "r_att", [128, 2, 7, 512], BF16)
        self.ybh = sbuf(nc, es, "r_ybh", [64, 2, 2048], F32)
        self.st2 = sbuf(nc, es, "r_st2", [64, 2], F32)
        self.red = sbuf(nc, es, "r_red", [1, 2], F32)
        self.sc = {k: sbuf(nc, es, f"r_sc_{k}", [1, 1], F32)[:]
                   for k in ("mean", "msq", "var", "lv", "rstd", "t1")}
        self.b64 = {k: sbuf(nc, es, f"r_b64_{k}", [64, 1], F32)[:]
                    for k in ("m", "rg")}
        self.gnw = sbuf(nc, es, "r_gnw", [64, 16], F32)
        self.gnb = sbuf(nc, es, "r_gnb", [64, 16], F32)
        self.ones64 = sbuf(nc, es, "r_ones64", [64, 1], F32)
        self.onesr = sbuf(nc, es, "r_onesr", [1, 64], F32)
        self.t1 = sbuf(nc, es, "r_t1", [64, 2048], F32)
        self.ynT = sbuf(nc, es, "r_ynT", [128, 8, 2048], BF16)
        self.wo = sbuf(nc, es, "r_wo", [128, 8, 1024], BF16)
        self.xr = sbuf(nc, es, "r_xr", [128, 2, 1024], F32)
        self.x1 = sbuf(nc, es, "r_x1", [128, 2, 1024], F32)
        self.psq = psum(nc, es, "r_psq", [128, 5, 512], F32)
        self.psy = psum(nc, es, "r_psy", [64, 2, 512], F32)
        self.psg = psum(nc, es, "r_psg", [64, 4], F32)




def emit_retention_batch(p, rx, b, qT_d, kT_d, v_d, xsrc, xdst, xsrc_key, xdst_key,
                         gam_pows):
    t0 = b * SG
    p.dma("gpsimd", "rq", rx.qTb[:],
          qT_d.rearrange("(kc p) t -> p kc t", p=128)[:, :, t0:t0+SG],
          writes=[("qTb",)], tag="ldq")
    p.dma("gpsimd", "rk", rx.kTb[:],
          kT_d.rearrange("(kc p) t -> p kc t", p=128)[:, :, t0:t0+SG],
          writes=[("kTb",)], tag="ldk")
    p.dma("gpsimd", "rv", rx.vb[:],
          v_d[t0:t0+SG, :].rearrange("(i p) n -> p i n", p=128),
          writes=[("vb",)], tag="ldv")

    DELAY = 4

    def emit_y(hp, g, ncs, cs):
        sl = cs % 7
        n0 = max(0, cs - 4 * g) * 128
        for idx in range(2):
            h = 2 * hp + idx
            p.emit("tensor", lambda t, h=h, idx=idx, cs=cs, sl=sl, ncs=ncs, n0=n0:
                   t.matmul(rx.psy[0:64, idx, n0:512],
                            rx.vb[:, cs, h*64:(h+1)*64],
                            rx.att[:, idx, sl, n0:512],
                            start=(cs == 0), stop=(cs == ncs - 1)),
                   reads=[("vb",), ("att", idx, sl)], writes=[("psy", idx)],
                   tag=f"y{hp}.{idx}.{g}.{cs}")

    qkctr = 0
    for hp in range(8):
        for g in range(4):
            ncs = 4 * g + 4
            for cs in range(ncs):
                nz = max(0, cs - 4 * g)
                n0 = nz * 128
                banks = []
                for idx in range(2):
                    bank = qkctr % 5
                    qkctr += 1
                    banks.append(bank)
                    hb = idx * 64
                    p.emit("tensor", lambda t, hb=hb, hp=hp, cs=cs, g=g, bank=bank,
                           n0=n0:
                           t.matmul(rx.psq[:, bank, n0:512],
                                    rx.kTb[hb:hb+64, hp, cs*128:(cs+1)*128],
                                    rx.qTb[hb:hb+64, hp, g*512+n0:(g+1)*512],
                                    start=True, stop=True),
                           reads=[("kTb",), ("qTb",)], writes=[("psq", bank)],
                           tag=f"qk{hp}.{idx}.{g}.{cs}")
                sl = cs % 7
                for idx in range(2):
                    h = 2 * hp + idx
                    bank = banks[idx]
                    for q in range(nz, 4):
                        ct = 4 * g + q
                        fd = 1 if cs == ct else 0
                        cconst = float(gam_pows[h] ** (128 * (ct - cs)) * 0.125)
                        p.emit("vector", lambda v, idx=idx, sl=sl, q=q,
                               bank=bank, fd=fd, cconst=cconst, h=h:
                               v.scalar_tensor_tensor(
                                   rx.att[:, idx, sl, q*128:(q+1)*128],
                                   rx.psq[:, bank, q*128:(q+1)*128], cconst,
                                   rx.gma[:, h, fd, :], ALU.mult, ALU.mult),
                               reads=[("psq", bank), ("gma",)],
                               writes=[("att", idx, sl)], tag="decay")
                if cs >= DELAY:
                    emit_y(hp, g, ncs, cs - DELAY)
            for cs in range(max(0, ncs - DELAY), ncs):
                emit_y(hp, g, ncs, cs)
            for idx in range(2):
                p.emit("scalar", lambda s, idx=idx, g=g: s.copy(
                    rx.ybh[:, idx, g*512:(g+1)*512], rx.psy[0:64, idx, :]),
                    reads=[("psy", idx)], writes=[("ybh", idx)], tag="yevac")
        for idx in range(2):
            emit_gn_head(p, rx, 2 * hp + idx, idx)

    emit_wo_residual(p, rx, b, xsrc, xdst, xsrc_key, xdst_key)


def emit_gn_head(p, rx, h, idx):
    hb = idx * 64
    hc = h // 2
    ybh = rx.ybh[:, idx, :]
    p.emit("vector", lambda v: v.reduce_sum(rx.st2[:, 0:1], ybh, axis=AX.X),
           reads=[("ybh", idx)], writes=[("st2",)])
    p.emit("vector", lambda v: v.scalar_tensor_tensor(
        rx.t1[:], ybh, 1.0, ybh, ALU.mult, ALU.mult,
        accum_out=rx.st2[:, 1:2]),
        reads=[("ybh", idx)], writes=[("st2",), ("t1",)])
    p.emit("tensor", lambda t: t.matmul(rx.psg[0:1, 0:2], rx.ones64[:],
                                        rx.st2[:], start=True, stop=True),
           reads=[("st2",), ("ones64",)], writes=[("psg",)], tag="gnred")
    p.emit("scalar", lambda s: s.copy(rx.red[:], rx.psg[0:1, 0:2]),
           reads=[("psg",)], writes=[("red",)])
    NT = 64 * 2048
    p.emit("scalar", lambda s: s.mul(rx.sc["mean"], rx.red[:, 0:1], 1.0 / NT),
           reads=[("red",)], writes=[("scmean",)])
    p.emit("scalar", lambda s: s.mul(rx.sc["msq"], rx.red[:, 1:2], 1.0 / NT),
           reads=[("red",)], writes=[("scmsq",)])
    p.emit("scalar", lambda s: s.activation(rx.sc["t1"], rx.sc["mean"], AF.Square),
           reads=[("scmean",)], writes=[("sct1",)])
    p.emit("vector", lambda v: v.tensor_sub(rx.sc["var"], rx.sc["msq"], rx.sc["t1"]),
           reads=[("scmsq",), ("sct1",)], writes=[("scvar",)])
    p.emit("scalar", lambda s: s.activation(rx.sc["lv"], rx.sc["var"], AF.Ln,
                                            bias=rx.cc.col("gneps")[0:1, :]),
           reads=[("scvar",), rx.cc.key("gneps")], writes=[("sclv",)])
    p.emit("scalar", lambda s: s.activation(rx.sc["rstd"], rx.sc["lv"], AF.Exp,
                                            scale=-0.5),
           reads=[("sclv",)], writes=[("scrstd",)])
    def mkb(t):
        t.matmul(rx.psg[0:64, 1:2], rx.onesr[:], rx.sc["mean"],
                 start=True, stop=True)
        return t.matmul(rx.psg[0:64, 2:3], rx.onesr[:], rx.sc["rstd"],
                        start=True, stop=True)
    p.emit("tensor", mkb, reads=[("scmean",), ("scrstd",), ("onesr",)],
           writes=[("psg",)], tag="gnbcast")
    p.emit("scalar", lambda s: s.copy(rx.b64["m"], rx.psg[0:64, 1:2]),
           reads=[("psg",)], writes=[("b64m",)])
    p.emit("vector", lambda v, h=h: v.tensor_scalar_mul(
        rx.b64["rg"], rx.psg[0:64, 2:3], rx.gnw[:, h:h+1]),
        reads=[("psg",), ("gnw",)], writes=[("b64rg",)])
    p.emit("vector", lambda v: v.tensor_scalar(
        rx.t1[:], ybh, rx.b64["m"], rx.b64["rg"], ALU.subtract, ALU.mult),
        reads=[("ybh", idx), ("b64m",), ("b64rg",)], writes=[("t1",)])
    p.emit("scalar", lambda s, hb=hb, hc=hc, h=h: s.activation(
        rx.ynT[hb:hb+64, hc, :], rx.t1[:], AF.Identity,
        bias=rx.gnb[:, h:h+1]),
        reads=[("t1",), ("gnb",)], writes=[("ynT", h)], tag="gnout")


def emit_wo_residual(p, rx, b, xsrc, xdst, xsrc_key, xdst_key):
    t0 = b * SG
    for i in range(NCHUNK):
        rows = slice(t0 + i * 128, t0 + (i + 1) * 128)
        xb = i % 2
        p.dma("sync", "lnx", rx.xr[:, xb, :], xsrc[rows, :],
              reads=[(xsrc_key, b, i)], writes=[("xr", xb)], tag="xres")
        for n in range(2):
            pp = (i * 2 + n) % 5
            def mko(t, i=i, n=n, pp=pp):
                last = None
                for kc in range(8):
                    last = t.matmul(rx.psq[:, pp, :],
                                    rx.ynT[:, kc, i*128:(i+1)*128],
                                    rx.wo[:, kc, n*512:(n+1)*512],
                                    start=(kc == 0), stop=(kc == 7))
                return last
            p.emit("tensor", mko,
                   reads=[("wo",)] + [("ynT", hh) for hh in range(16)],
                   writes=[("psq", pp)], tag=f"wo{i}.{n}")
            p.emit("vector", lambda v, i=i, n=n, pp=pp, xb=xb: v.tensor_add(
                rx.x1[:, xb, n*512:(n+1)*512], rx.psq[:, pp, :],
                rx.xr[:, xb, n*512:(n+1)*512]),
                reads=[("psq", pp), ("xr", xb)], writes=[("x1", xb)], tag="resadd")
        p.dma("sync", "xo", xdst[rows, :], rx.x1[:, xb, :],
              reads=[("x1", xb)], writes=[(xdst_key, b, i)], tag="x1out")







def pack_ffn_weights(Wfc_r, Wproj_r, Wfc_sh, Wproj_sh):
    fcs = [Wfc_r[e] for e in range(8)] + [Wfc_sh[0], Wfc_sh[1]]
    prs = [Wproj_r[e] for e in range(8)] + [Wproj_sh[0], Wproj_sh[1]]
    out = np.empty((NE, NPIECE, 128, 16384), np.float32)
    for e in range(NE):
        A = np.asarray(fcs[e])
        arr = A.reshape(4, 8, 128, 8, 128).transpose(0, 4, 1, 3, 2)
        out[e, :, :, 0:8192] = arr.reshape(4, 128, 8192)
        P = np.asarray(prs[e])
        arr2 = P.reshape(1024, 4, 8, 128).transpose(1, 3, 2, 0)
        out[e, :, :, 8192:16384] = arr2.reshape(4, 128, 8192)
    return out


class FfnCtx:
    def __init__(self, p, es):
        nc = p.nc
        self.acc = sbuf(nc, es, "ffn_acc", [128, 16, 1024], F32)
        self.xlnT = sbuf(nc, es, "ffn_xlnT", [128, 8, 2048], BF16)
        self.xln_tm = sbuf(nc, es, "ffn_xlntm", [128, 2, 1024], BF16)
        self.wp = sbuf(nc, es, "ffn_wp", [128, 2, 16384], BF16)
        self.h = sbuf(nc, es, "ffn_h", [128, 8, 512], BF16)
        self.sq = sbuf(nc, es, "ffn_sq", [128, 16], F32)
        self.sm = sbuf(nc, es, "ffn_sm", [128, 16], F32)
        self.mn = sbuf(nc, es, "ffn_mn", [128, 16], F32)
        self.m2t = sbuf(nc, es, "ffn_m2t", [128, 16], F32)
        self.var = sbuf(nc, es, "ffn_var", [128, 16], F32)
        self.lv = sbuf(nc, es, "ffn_lv", [128, 16], F32)
        self.rstd = sbuf(nc, es, "ffn_rstd", [128, 16], F32)
        self.r1 = sbuf(nc, es, "ffn_r1", [128, 16], F32)
        self.mr = sbuf(nc, es, "ffn_mr", [128, 16], F32)
        self.junk = sbuf(nc, es, "ffn_junk", [128, 1024], F32)
        self.cmb = sbuf(nc, es, "ffn_cmb", [128, 16, 8], F32)
        self.rt = {k: sbuf(nc, es, f"ffn_rt_{k}", [128, 8], F32)[:]
                   for k in ("balin", "bal", "sco", "bal2", "mask1", "mask2", "j8")}
        self.rc = {k: sbuf(nc, es, f"ffn_rc_{k}", [128, 1], F32)[:]
                   for k in ("m1", "m2", "s1", "s2", "wa", "wb", "neg")}
        self.rbias = sbuf(nc, es, "ffn_rbias", [128, 8], F32)
        self.centT = sbuf(nc, es, "ffn_centT", [128, 8, 8], BF16)
        self.ps_h = psum(nc, es, "ffn_ps_h", [128, 2, 512], F32)
        self.ps_o = psum(nc, es, "ffn_ps_o", [128, 3, 512], F32)
        self.ps_r = psum(nc, es, "ffn_ps_r", [128, 8], F32)
        self.ps_t = psum(nc, es, "ffn_ps_t", [128, 2, 1024], BF16)

    def load_consts(self, p, centT_dram, rbias_dram):
        p.dma("gpsimd", "c", self.centT[:],
              centT_dram.rearrange("(kc p) e -> p kc e", p=128),
              writes=[("centT",)], tag="centT")
        p.dma("gpsimd", "c", self.rbias[:], rbias_dram, writes=[("rbias",)], tag="rbias")


def emit_ffn_call(p, fx, xsrc, xdst, call_id, xsrc_key, xdst_key):
    nc = p.nc
    wffn = fx.wffn_ap
    pieces = [(e, pc) for e in range(NE) for pc in range(NPIECE)]

    for sg in range(2):
        t0 = sg * SG
        for i in range(NCHUNK):
            rows = slice(t0 + i * 128, t0 + (i + 1) * 128)
            acc_i = fx.acc[:, i, :]
            p.dma("sync", "accin", acc_i, xsrc[rows, :],
                  reads=[(xsrc_key, sg, i)], writes=[("acc", i)], tag=f"accin{call_id}")
            p.emit("vector", lambda v, a=acc_i, i=i: v.scalar_tensor_tensor(
                fx.junk[:], a, 1.0, a, ALU.mult, ALU.mult, accum_out=fx.sq[:, i:i+1]),
                reads=[("acc", i)], writes=[("sq",), ("junk",)], tag="sumsq")
            p.emit("vector", lambda v, a=acc_i, i=i: v.reduce_sum(
                fx.sm[:, i:i+1], a, axis=AX.X),
                reads=[("acc", i)], writes=[("sm",)], tag="sum")
        p.emit("scalar", lambda s: s.mul(fx.mn[:], fx.sm[:], 1.0 / D),
               reads=[("sm",)], writes=[("mn",)])
        p.emit("scalar", lambda s: s.activation(fx.m2t[:], fx.mn[:], AF.Square),
               reads=[("mn",)], writes=[("m2t",)])
        p.emit("vector", lambda v: v.scalar_tensor_tensor(
            fx.var[:], fx.sq[:], 1.0 / D, fx.m2t[:], ALU.mult, ALU.subtract),
            reads=[("sq",), ("m2t",)], writes=[("var",)])
        p.emit("scalar", lambda s: s.activation(fx.lv[:], fx.var[:], AF.Ln,
                                                bias=fx.cc.col("eps")),
               reads=[("var",), fx.cc.key("eps")], writes=[("lv",)])
        p.emit("scalar", lambda s: s.activation(fx.rstd[:], fx.lv[:], AF.Exp, scale=-0.5),
               reads=[("lv",)], writes=[("rstd",)])
        p.emit("scalar", lambda s: s.add(fx.r1[:], fx.rstd[:], 1.0),
               reads=[("rstd",)], writes=[("r1",)])
        p.emit("vector", lambda v: v.tensor_mul(fx.mr[:], fx.mn[:], fx.rstd[:]),
               reads=[("mn",), ("rstd",)], writes=[("mr",)])

        for i in range(NCHUNK):
            b = i % 2
            xt = fx.xln_tm[:, b, :]
            p.emit("vector", lambda v, i=i, xt=xt: v.tensor_scalar(
                xt, fx.acc[:, i, :], fx.mn[:, i:i+1], fx.rstd[:, i:i+1],
                ALU.subtract, ALU.mult),
                reads=[("acc", i), ("mn",), ("rstd",)], writes=[("xlntm", b)], tag="ln")
            p.emit("vector", lambda v, i=i: v.tensor_scalar(
                fx.acc[:, i, :], fx.acc[:, i, :], fx.r1[:, i:i+1], fx.mr[:, i:i+1],
                ALU.mult, ALU.subtract),
                reads=[("acc", i), ("r1",), ("mr",)], writes=[("acc", i)], tag="lnres")
            for kc in range(8):
                j = kc % 2
                p.emit("tensor", lambda t, kc=kc, j=j, xt=xt: t.transpose(
                    fx.ps_t[:, j, 0:128], xt[:, kc*128:(kc+1)*128], p.idbf),
                    reads=[("xlntm", b), ("idbf",)], writes=[("pst", j)],
                    tag="lnT")
                p.emit("scalar", lambda s, kc=kc, j=j, i=i: s.copy(
                    fx.xlnT[:, kc, i*128:(i+1)*128], fx.ps_t[:, j, 0:128]),
                    reads=[("pst", j)], writes=[("xlnT",)], tag="lnTe")
            def mk_raw(t, i=i):
                last = None
                for kc in range(8):
                    last = t.matmul(fx.ps_r[:], fx.xlnT[:, kc, i*128:(i+1)*128],
                                    fx.centT[:, kc, :], start=(kc == 0), stop=(kc == 7))
                return last
            p.emit("tensor", mk_raw, reads=[("xlnT",), ("centT",)], writes=[("psr",)],
                   tag="rawmm")
            rt, rc = fx.rt, fx.rc
            p.emit("vector", lambda v: v.tensor_add(rt["balin"], fx.ps_r[:], fx.rbias[:]),
                   reads=[("psr",), ("rbias",)], writes=[("balin",)])
            p.emit("scalar", lambda s: s.activation(rt["bal"], rt["balin"], AF.Sigmoid),
                   reads=[("balin",)], writes=[("bal",)])
            p.emit("scalar", lambda s: s.activation(rt["sco"], fx.ps_r[:], AF.Sigmoid),
                   reads=[("psr",)], writes=[("sco",)])
            p.emit("vector", lambda v: v.reduce_max(rc["m1"], rt["bal"], axis=AX.X),
                   reads=[("bal",)], writes=[("m1",)])
            p.emit("vector", lambda v: v.tensor_scalar(
                rt["mask1"], rt["bal"], rc["m1"], None, ALU.is_ge),
                reads=[("bal",), ("m1",)], writes=[("mask1",)])
            p.emit("vector", lambda v: v.scalar_tensor_tensor(
                rt["bal2"], rt["mask1"], -1e30, rt["bal"], ALU.mult, ALU.add),
                reads=[("mask1",), ("bal",)], writes=[("bal2",)])
            p.emit("vector", lambda v: v.reduce_max(rc["m2"], rt["bal2"], axis=AX.X),
                   reads=[("bal2",)], writes=[("m2",)])
            p.emit("vector", lambda v: v.tensor_scalar(
                rt["mask2"], rt["bal2"], rc["m2"], None, ALU.is_ge),
                reads=[("bal2",), ("m2",)], writes=[("mask2",)])
            p.emit("vector", lambda v: v.scalar_tensor_tensor(
                rt["j8"], rt["sco"], 1.0, rt["mask1"], ALU.mult, ALU.mult,
                accum_out=rc["s1"]),
                reads=[("sco",), ("mask1",)], writes=[("s1",), ("j8",)])
            p.emit("vector", lambda v: v.scalar_tensor_tensor(
                rt["j8"], rt["sco"], 1.0, rt["mask2"], ALU.mult, ALU.mult,
                accum_out=rc["s2"]),
                reads=[("sco",), ("mask2",)], writes=[("s2",), ("j8",)])
            p.emit("scalar", lambda s: s.mul(rc["neg"], rc["s2"], -1.0),
                   reads=[("s2",)], writes=[("neg",)])
            p.emit("scalar", lambda s: s.activation(rc["wa"], rc["s1"], AF.Sigmoid,
                                                    bias=rc["neg"]),
                   reads=[("s1",), ("neg",)], writes=[("wa",)])
            p.emit("scalar", lambda s: s.mul(rc["neg"], rc["s1"], -1.0),
                   reads=[("s1",), ("wa",)], writes=[("neg",)])
            p.emit("scalar", lambda s: s.activation(rc["wb"], rc["s2"], AF.Sigmoid,
                                                    bias=rc["neg"]),
                   reads=[("s2",), ("neg",)], writes=[("wb",)])
            p.emit("vector", lambda v: v.tensor_scalar(
                rt["j8"], rt["mask1"], rc["wa"], None, ALU.mult),
                reads=[("mask1",), ("wa",)], writes=[("j8",)])
            p.emit("vector", lambda v, i=i: v.scalar_tensor_tensor(
                fx.cmb[:, i, :], rt["mask2"], rc["wb"], rt["j8"], ALU.mult, ALU.add),
                reads=[("mask2",), ("wb",), ("j8",)], writes=[("cmb", i)], tag="cmb")

        qctr = 0
        for pi, (e, pc) in enumerate(pieces):
            wbuf = pi % 2
            if pi == 0:
                p.dma("gpsimd", "wp", fx.wp[:, 0, :], wffn[e, pc, :, :],
                      reads=[], writes=[("wp", 0)], tag="wdma")
            if pi + 1 < len(pieces):
                e2, pc2 = pieces[pi + 1]
                p.dma("gpsimd", "wp", fx.wp[:, (pi + 1) % 2, :], wffn[e2, pc2, :, :],
                      reads=[], writes=[("wp", (pi + 1) % 2)], tag="wdma")
            for g in range(4):
                gs = slice(g * 512, (g + 1) * 512)
                for f in range(8):
                    pp = f % 2
                    def mk_fc(t, f=f, pp=pp, gs=gs, wbuf=wbuf):
                        last = None
                        for kc in range(8):
                            last = t.matmul(
                                fx.ps_h[:, pp, :],
                                fx.wp[:, wbuf, f*1024 + kc*128: f*1024 + (kc+1)*128],
                                fx.xlnT[:, kc, gs],
                                start=(kc == 0), stop=(kc == 7))
                        return last
                    p.emit("tensor", mk_fc,
                           reads=[("wp", wbuf), ("xlnT",)], writes=[("psh", pp)],
                           tag=f"fc{e}.{pc}.{g}.{f}")
                    p.emit("scalar", lambda s, f=f, pp=pp: s.activation(
                        fx.h[:, f, :], fx.ps_h[:, pp, :], AF.Gelu),
                        reads=[("psh", pp)], writes=[("hsb", f)], tag="gelu")
                for tsub in range(4):
                    for dh in range(2):
                        q = qctr % 3
                        qctr += 1
                        for half in range(2):
                            frange = range(half * 4, half * 4 + 4)
                            def mk_pj(t, frange=frange, tsub=tsub, dh=dh, q=q,
                                      wbuf=wbuf, half=half):
                                last = None
                                for f in frange:
                                    last = t.matmul(
                                        fx.ps_o[:, q, :],
                                        fx.h[:, f, tsub*128:(tsub+1)*128],
                                        fx.wp[:, wbuf,
                                              8192 + f*1024 + dh*512:
                                              8192 + f*1024 + (dh+1)*512],
                                        start=(half == 0 and f == frange[0]),
                                        stop=(half == 1 and f == frange[-1]))
                                return last
                            p.emit("tensor", mk_pj,
                                   reads=[("wp", wbuf)] + [("hsb", f) for f in frange],
                                   writes=[("pso", q)], tag=f"pj{e}.{pc}.{g}.{tsub}.{dh}")
                        i = g * 4 + tsub
                        accs = fx.acc[:, i, dh*512:(dh+1)*512]
                        if e < 8:
                            p.emit("vector", lambda v, q=q, accs=accs, i=i, e=e:
                                   v.scalar_tensor_tensor(
                                       accs, fx.ps_o[:, q, :], fx.cmb[:, i, e:e+1],
                                       accs, ALU.mult, ALU.add),
                                   reads=[("pso", q), ("cmb", i), ("acc", i)],
                                   writes=[("acc", i)], tag="evac")
                        else:
                            p.emit("vector", lambda v, q=q, accs=accs:
                                   v.tensor_add(accs, fx.ps_o[:, q, :], accs),
                                   reads=[("pso", q), ("acc", i)],
                                   writes=[("acc", i)], tag="evac")
        for i in range(NCHUNK):
            rows = slice(t0 + i * 128, t0 + (i + 1) * 128)
            p.dma("sync", "xo", xdst[rows, :], fx.acc[:, i, :],
                  reads=[("acc", i)], writes=[(xdst_key, sg, i)], tag="xout")







def pack_fc_blob(W, npiece):
    A = np.asarray(W)
    F = A.shape[0]
    arr = A.reshape(npiece, 8, 128, 8, 128).transpose(0, 4, 1, 3, 2)
    return np.ascontiguousarray(arr.reshape(npiece, 128, 8192))


class MbCtx:
    def __init__(self, p, es):
        nc = p.nc
        self.win = sbuf(nc, es, "m_win", [128, 2, 8192], BF16)
        self.wbc = sbuf(nc, es, "m_wbc", [128, 16, 96], BF16)
        self.cw = sbuf(nc, es, "m_cw", [128, 16, 4], F32)
        self.cb = sbuf(nc, es, "m_cb", [128, 16], F32)
        self.dd = sbuf(nc, es, "m_dd", [128, 16], F32)
        self.xi = sbuf(nc, es, "m_xi", [128, 2, 2052], BF16)
        self.u = sbuf(nc, es, "m_u", [128, 1, 2048], F32)
        self.c0 = self.u
        self.ubf = sbuf(nc, es, "m_ubf", [128, 1, 2048], BF16)
        self.uD = sbuf(nc, es, "m_uD", [128, 2048], F32)
        self.sz = sbuf(nc, es, "m_sz", [128, 2, 512], BF16)
        self.utc = sbuf(nc, es, "m_utc", [128, 1, 16, 128], BF16)
        self.dpre = sbuf(nc, es, "m_dpre", [16, 2048], F32)
        self.da = sbuf(nc, es, "m_da", [16, 2048], F32)
        self.db = self.dpre
        self.ps_xz = psum(nc, es, "m_psxz", [128, 2, 512], F32)
        self.ps_bc = psum(nc, es, "m_psbc", [96, 2048], F32)
        self.ps_tp = psum(nc, es, "m_pstp", [128, 2, 1024], BF16)


class ScanShared:
    def __init__(self, p, es):
        nc = p.nc
        self.wout = sbuf(nc, es, "s_wout", [128, 16, 1024], BF16)
        self.oneh = sbuf(nc, es, "s_oneh", [16, 16, 128], F32)
        self.kmask = sbuf(nc, es, "s_kmask", [128, 128], F32)
        self.idf = sbuf(nc, es, "s_idf", [128, 128], F32)
        self.zc16 = sbuf(nc, es, "s_zc16", [16, 1], F32)


class ScanBatch:
    def __init__(self, p, es, sh, b, stash, xsrc, xdst, xsrc_key, xdst_key):
        nc = p.nc
        self.p, self.sh, self.b = p, sh, b
        self.stash = stash
        self.xsrc, self.xdst = xsrc, xdst
        self.xsrc_key, self.xdst_key = xsrc_key, xdst_key
        pre = f"sb{b}"
        self.utc = sbuf(nc, es, pre + "_utc", [128, 2, 16, 128], BF16)
        self.uDc = sbuf(nc, es, pre + "_uDc", [128, 2, 16, 128], BF16)
        self.szc = sbuf(nc, es, pre + "_szc", [128, 16, 128], BF16)
        self.G = sbuf(nc, es, pre + "_G", [128, 2, 128], F32)
        self.t2 = sbuf(nc, es, pre + "_t2", [128, 2, 128], F32)
        self.kacc = sbuf(nc, es, pre + "_kacc", [128, 2, 128], F32)
        self.kbf = sbuf(nc, es, pre + "_kbf", [128, 128], BF16)
        self.dBT = sbuf(nc, es, pre + "_dBT", [128, 16], F32)
        self.ST = sbuf(nc, es, pre + "_ST", [128, 16], F32)
        self.negST = sbuf(nc, es, pre + "_negST", [128, 16], F32)
        self.E = sbuf(nc, es, pre + "_E", [16, 128], F32)
        self.EdB = sbuf(nc, es, pre + "_EdB", [16, 128], F32)
        self.EdBT = sbuf(nc, es, pre + "_EdBT", [128, 16], BF16)
        self.P = sbuf(nc, es, pre + "_P", [16, 128], F32)
        self.CP = sbuf(nc, es, pre + "_CP", [16, 128], BF16)
        self.ptot = sbuf(nc, es, pre + "_ptot", [16, 1], F32)
        self.t16 = sbuf(nc, es, pre + "_t16", [16, 128], F32)
        self.ys = sbuf(nc, es, pre + "_ys", [128, 16, 128], BF16)
        self.hst = sbuf(nc, es, pre + "_hst", [16, 2048], F32)
        self.hstbf = sbuf(nc, es, pre + "_hstbf", [16, 2048], BF16)
        self.xr = sbuf(nc, es, pre + "_xr", [128, 1024], F32)
        self.x3 = sbuf(nc, es, pre + "_x3", [128, 1024], F32)
        self.psA = psum(nc, es, pre + "_psA", [128, 2, 512], F32)
        self.psY = psum(nc, es, pre + "_psY", [128, 4, 128], F32)
        self.psW = psum(nc, es, pre + "_psW", [128, 512], F32)

    def K(self, *a):
        return ("s", self.b) + a

    def init_emit(self):
        p, K = self.p, self.K
        p.emit("vector", lambda v: v.memset(self.hst[:], 0.0),
               writes=[K("hst")], tag="hini")
        p.emit("vector", lambda v: v.memset(self.hstbf[:], 0.0),
               writes=[K("hstbf")], tag="hini2")

    def prep_emit(self, c):
        p, sh, b, K = self.p, self.sh, self.b, self.K
        sx = self
        cb = c % 2
        t0 = b * SG
        S, Cm, dB = self.stash["S"], self.stash["Cm"], self.stash["dB"]
        utc_d, uD_d, sz_d = self.stash["utc"], self.stash["uD"], self.stash["sz"]
        trange = slice(t0 + c*128, t0 + (c+1)*128)
        p.dma("gpsimd", f"sci{b}", sx.utc[:, cb, :, :],
              utc_d[:, trange, :].rearrange("cc t n -> t cc n"),
              writes=[K("utc", cb)], tag="utci")
        p.dma("gpsimd", f"sci{b}", sx.uDc[:, cb, :, :],
              uD_d[:, trange].rearrange("(cc p) t -> p cc t", p=128),
              writes=[K("uDc", cb)], tag="uDi")
        p.dma("gpsimd", f"scz{b}", sx.szc[:],
              sz_d[:, trange].rearrange("(cc p) t -> p cc t", p=128),
              writes=[K("szc")], tag="szi")
        Sc = S[:, c*128:(c+1)*128]
        Cc = Cm[:, c*128:(c+1)*128]
        dBc = dB[:, c*128:(c+1)*128]
        send = S[:, c*128+127: c*128+128]
        sprev = sh.zc16[:] if c == 0 else S[:, c*128-1: c*128]

        p.emit("tensor", lambda t, dBc=dBc: t.transpose(
            sx.psA[:, 0, 0:16], dBc, sh.idf[0:16, 0:16]),
            reads=[("stash", b, "p2"), ("idf",)], writes=[K("psA", 0)], tag="dBT")
        p.emit("vector", lambda v: v.tensor_copy(sx.dBT[:], sx.psA[:, 0, 0:16]),
               reads=[K("psA", 0)], writes=[K("dBT")])
        p.emit("tensor", lambda t, Sc=Sc: t.transpose(
            sx.psA[:, 1, 0:16], Sc, sh.idf[0:16, 0:16]),
            reads=[("stash", b, "p2"), ("idf",)], writes=[K("psA", 1)], tag="STt")
        p.emit("vector", lambda v: v.tensor_copy(sx.ST[:], sx.psA[:, 1, 0:16]),
               reads=[K("psA", 1)], writes=[K("ST")])
        p.emit("vector", lambda v, Sc=Sc, send=send: v.tensor_scalar(
            sx.E[:], Sc, send, -1.0, ALU.subtract, ALU.mult),
            reads=[("stash", b, "p2")], writes=[K("E")])
        p.emit("scalar", lambda s: s.activation(sx.E[:], sx.E[:], AF.Exp),
               reads=[K("E")], writes=[K("E")])
        p.emit("vector", lambda v, dBc=dBc: v.tensor_mul(sx.EdB[:], sx.E[:], dBc),
               reads=[K("E"), ("stash", b, "p2")], writes=[K("EdB")])
        p.emit("tensor", lambda t: t.transpose(
            sx.psA[:, 0, 0:16], sx.EdB[:], sh.idf[0:16, 0:16]),
            reads=[K("EdB"), ("idf",)], writes=[K("psA", 0)], tag="EdBT")
        p.emit("scalar", lambda s: s.copy(sx.EdBT[:], sx.psA[:, 0, 0:16]),
               reads=[K("psA", 0)], writes=[K("EdBT")])
        p.emit("vector", lambda v, Sc=Sc, sprev=sprev: v.tensor_scalar(
            sx.P[:], Sc, sprev, None, ALU.subtract),
            reads=[("stash", b, "p2"), ("zc16",)], writes=[K("P")])
        p.emit("scalar", lambda s: s.activation(sx.P[:], sx.P[:], AF.Exp),
               reads=[K("P")], writes=[K("P")])
        p.emit("vector", lambda v, Cc=Cc: v.tensor_mul(sx.CP[:], Cc, sx.P[:]),
               reads=[("stash", b, "p1"), K("P")], writes=[K("CP")])
        p.emit("vector", lambda v, send=send, sprev=sprev: v.tensor_scalar(
            sx.t16[:, 0:1], send, sprev, None, ALU.subtract),
            reads=[("stash", b, "p2"), ("zc16",)], writes=[K("t16")])
        p.emit("scalar", lambda s: s.activation(sx.ptot[:], sx.t16[:, 0:1], AF.Exp),
               reads=[K("t16")], writes=[K("ptot")])

        for n in range(NST):
            ja = n % 2
            jc = (n + 1) % 2
            p.emit("tensor", lambda t, n=n, ja=ja, Sc=Sc: t.matmul(
                sx.psA[:, ja, 0:128], sh.oneh[:, n, :], Sc,
                start=True, stop=True),
                reads=[("oneh",), ("stash", b, "p2")],
                writes=[K("psA", ja)], tag=f"D{n}")
            p.emit("vector", lambda v, ja=ja, n=n: v.tensor_scalar(
                sx.t2[:, ja, :], sx.psA[:, ja, 0:128], sx.ST[:, n:n+1], 0.0,
                ALU.subtract, ALU.min),
                reads=[K("psA", ja), K("ST")], writes=[K("t2", ja)], tag="Dclamp")
            p.emit("scalar", lambda s, ja=ja: s.activation(
                sx.G[:, ja, :], sx.t2[:, ja, :], AF.Exp),
                reads=[K("t2", ja)], writes=[K("G", ja)], tag="Gexp")
            p.emit("tensor", lambda t, n=n, jc=jc, Cc=Cc: t.matmul(
                sx.psA[:, jc, 0:128], sh.oneh[:, n, :], Cc,
                start=True, stop=True),
                reads=[("oneh",), ("stash", b, "p1")],
                writes=[K("psA", jc)], tag=f"C{n}")
            p.emit("vector", lambda v, ja=ja, jc=jc: v.tensor_mul(
                sx.t2[:, ja, :], sx.G[:, ja, :], sx.psA[:, jc, 0:128]),
                reads=[K("G", ja), K("psA", jc)], writes=[K("t2", ja)], tag="GC")
            par = n % 2
            if n < 2:
                p.emit("vector", lambda v, ja=ja, n=n, par=par: v.tensor_scalar_mul(
                    sx.kacc[:, par, :], sx.t2[:, ja, :], sx.dBT[:, n:n+1]),
                    reads=[K("t2", ja), K("dBT")], writes=[K("kacc", par)], tag="K0")
            else:
                p.emit("vector", lambda v, ja=ja, n=n, par=par: v.scalar_tensor_tensor(
                    sx.kacc[:, par, :], sx.t2[:, ja, :], sx.dBT[:, n:n+1],
                    sx.kacc[:, par, :], ALU.mult, ALU.add),
                    reads=[K("t2", ja), K("dBT"), K("kacc", par)],
                    writes=[K("kacc", par)], tag="Kn")
        p.emit("vector", lambda v: v.scalar_tensor_tensor(
            sx.kacc[:, 0, :], sx.kacc[:, 1, :], 1.0, sx.kacc[:, 0, :],
            ALU.mult, ALU.add),
            reads=[K("kacc", 0), K("kacc", 1)], writes=[K("kacc", 0)], tag="Ksum")
        p.emit("vector", lambda v: v.tensor_mul(sx.kbf[:], sx.kacc[:, 0, :],
                                                sh.kmask[:]),
               reads=[K("kacc", 0), ("kmask",)], writes=[K("kbf")], tag="Kmask")

    def rest_emit(self, c):
        p, sh, b, K = self.p, self.sh, self.b, self.K
        sx = self
        cb = c % 2
        t0 = b * SG
        trange = slice(t0 + c*128, t0 + (c+1)*128)
        for ps4 in range(4):
            for j in range(4):
                chc = ps4 * 4 + j
                def mky(t, chc=chc, j=j, cb=cb):
                    t.matmul(sx.psY[:, j, :], sx.utc[:, cb, chc, :], sx.kbf[:],
                             start=True, stop=False)
                    return t.matmul(sx.psY[:, j, :],
                                    sx.hstbf[:, chc*128:(chc+1)*128],
                                    sx.CP[:], start=False, stop=True)
                p.emit("tensor", mky,
                       reads=[K("utc", cb), K("kbf"), K("hstbf"), K("CP")],
                       writes=[K("psY")], tag=f"y{chc}")
            sl = slice(ps4*4*128, (ps4+1)*4*128)
            p.emit("vector", lambda v, ps4=ps4, sl=sl, cb=cb: v.tensor_add(
                sx.ys[:].rearrange("p c n -> p (c n)")[:, sl],
                sx.psY[:].rearrange("p c n -> p (c n)"),
                sx.uDc[:, cb, :, :].rearrange("p c n -> p (c n)")[:, sl]),
                reads=[K("psY"), K("uDc", cb)], writes=[K("ys")], tag="yadd")
            p.emit("vector", lambda v, sl=sl: v.tensor_mul(
                sx.ys[:].rearrange("p c n -> p (c n)")[:, sl],
                sx.ys[:].rearrange("p c n -> p (c n)")[:, sl],
                sx.szc[:].rearrange("p c n -> p (c n)")[:, sl]),
                reads=[K("ys"), K("szc")], writes=[K("ys")], tag="ymul")
        for m in range(4):
            def mkh(t, m=m, cb=cb):
                return t.matmul(sx.psW[0:16, :], sx.EdBT[:],
                                sx.utc[:, cb, m*4:(m+1)*4, :],
                                start=True, stop=True)
            p.emit("tensor", mkh, reads=[K("EdBT"), K("utc", cb)],
                   writes=[K("psW")], tag=f"h{m}")
            p.emit("vector", lambda v, m=m: v.scalar_tensor_tensor(
                sx.hst[:, m*512:(m+1)*512], sx.hst[:, m*512:(m+1)*512],
                sx.ptot[:], sx.psW[0:16, :], ALU.mult, ALU.add),
                reads=[K("psW"), K("ptot"), K("hst")], writes=[K("hst")],
                tag="hupd")
        p.emit("scalar", lambda s: s.copy(sx.hstbf[:], sx.hst[:]),
               reads=[K("hst")], writes=[K("hstbf")], tag="hbf")
        p.dma("sync", f"lnx{b}", sx.xr[:], self.xsrc[trange, :],
              reads=[(self.xsrc_key, b, c)], writes=[K("xr")], tag="xr2")
        for dh in range(2):
            def mko(t, dh=dh):
                last = None
                for cc2 in range(16):
                    last = t.matmul(sx.psW[:], sx.ys[:, cc2, :],
                                    sh.wout[:, cc2, dh*512:(dh+1)*512],
                                    start=(cc2 == 0), stop=(cc2 == 15))
                return last
            p.emit("tensor", mko, reads=[K("ys"), ("wout",)],
                   writes=[K("psW")], tag=f"wo{dh}")
            p.emit("vector", lambda v, dh=dh: v.tensor_add(
                sx.x3[:, dh*512:(dh+1)*512], sx.psW[:],
                sx.xr[:, dh*512:(dh+1)*512]),
                reads=[K("psW"), K("xr")], writes=[K("x3")], tag="res3")
        p.dma("sync", f"xo{b}", self.xdst[trange, :], sx.x3[:],
              reads=[K("x3")], writes=[(self.xdst_key, b, c)], tag="x3o")


def emit_mamba_b2(p, lx, mx, b, xsrc, xsrc_key, stash):
    utc_d, uD_d, sz_d = stash["utc"], stash["uD"], stash["sz"]
    emit_ln_sg(p, lx, mx.cc, xsrc, b, xsrc_key)
    t0 = b * SG
    STQ = ("gpsimd", "gpsimd", "gpsimd")
    stoc = [0]

    def sto_dma(out, in_, reads, writes, tag):
        eng = STQ[stoc[0] % 3]
        stoc[0] += 1
        p.dma(eng, "sto_" + eng, out, in_, reads=reads, writes=writes, tag=tag)

    for cc in range(32):
        xib = cc % 2
        ub = 0
        pi, f = cc // 8, cc % 8
        if f == 0:
            if pi == 0:
                p.dma("gpsimd", "wp", mx.win[:, 0, :], mx.win_d[pi, :, :],
                      writes=[("win", 0)], tag="winp")
            if pi + 1 < 4:
                p.dma("gpsimd", "wp", mx.win[:, (pi + 1) % 2, :],
                      mx.win_d[pi + 1, :, :],
                      writes=[("win", (pi + 1) % 2)], tag="winp")
        for g in range(4):
            pp = g % 2
            def mkxz(t, pi=pi, f=f, g=g, pp=pp):
                last = None
                for kc in range(8):
                    last = t.matmul(
                        mx.ps_xz[:, pp, :],
                        mx.win[:, pi % 2, f*1024 + kc*128: f*1024 + (kc+1)*128],
                        lx.xlnT[:, kc, g*512:(g+1)*512],
                        start=(kc == 0), stop=(kc == 7))
                return last
            p.emit("tensor", mkxz, reads=[("win", pi % 2), ("xlnT",)],
                   writes=[("psxz", pp)], tag=f"xz{cc}.{g}")
            if cc < 16:
                p.emit("vector", lambda v, g=g, pp=pp, xib=xib: v.tensor_copy(
                    mx.xi[:, xib, 3 + g*512: 3 + (g+1)*512], mx.ps_xz[:, pp, :]),
                    reads=[("psxz", pp)], writes=[("xi", xib)], tag="xiev")
            else:
                zb = g % 2
                p.emit("scalar", lambda s, g=g, pp=pp, zb=zb: s.activation(
                    mx.sz[:, zb, :], mx.ps_xz[:, pp, :], AF.Silu),
                    reads=[("psxz", pp)], writes=[("szrow", zb)], tag="siluz")
                sto_dma(sz_d[(cc-16)*128:(cc-16+1)*128,
                             t0 + g*512: t0 + (g+1)*512],
                        mx.sz[:, zb, :],
                        reads=[("szrow", zb)], writes=[("szst", b, cc, g)],
                        tag="szo")
        if cc < 16:
            u_ = mx.u[:, ub, :]
            uD_ = mx.uD[:]
            ubf_ = mx.ubf[:, ub, :]
            p.emit("vector", lambda v, cc=cc, xib=xib, u_=u_: v.tensor_scalar_mul(
                u_, mx.xi[:, xib, 0:2048], mx.cw[:, cc, 0:1]),
                reads=[("xi", xib), ("cw",)], writes=[("u", ub)], tag="cv0")
            for j in (1, 2):
                p.emit("vector", lambda v, j=j, cc=cc, xib=xib, u_=u_:
                       v.scalar_tensor_tensor(
                    u_, mx.xi[:, xib, j:j+2048], mx.cw[:, cc, j:j+1],
                    u_, ALU.mult, ALU.add),
                    reads=[("xi", xib), ("cw",), ("u", ub)], writes=[("u", ub)],
                    tag=f"cv{j}")
            p.emit("vector", lambda v, cc=cc, xib=xib, u_=u_:
                   v.scalar_tensor_tensor(
                u_, mx.xi[:, xib, 3:3+2048], mx.cw[:, cc, 3:4],
                u_, ALU.mult, ALU.add),
                reads=[("xi", xib), ("cw",), ("u", ub)], writes=[("u", ub)],
                tag="cv3")
            p.emit("scalar", lambda s, cc=cc, u_=u_: s.activation(
                u_, u_, AF.Silu, bias=mx.cb[:, cc:cc+1]),
                reads=[("u", ub), ("cb",)], writes=[("u", ub)], tag="silu_u")
            p.emit("vector", lambda v, cc=cc, u_=u_, uD_=uD_: v.tensor_scalar_mul(
                uD_, u_, mx.dd[:, cc:cc+1]),
                reads=[("u", ub), ("dd",)], writes=[("uDr",)], tag="uD")
            p.emit("scalar", lambda s, u_=u_, ubf_=ubf_: s.copy(ubf_, u_),
                   reads=[("u", ub)], writes=[("ubf", ub)], tag="ubf")
            p.dma("gpsimd", "sto_gpsimd", uD_d[cc*128:(cc+1)*128, t0:t0+SG],
                  uD_, reads=[("uDr",)], writes=[("uDst", b, cc)], tag="uDo")
            def mkbc(t, cc=cc, ubf_=ubf_):
                last = None
                for g in range(4):
                    last = t.matmul(mx.ps_bc[:, g*512:(g+1)*512],
                                    mx.wbc[:, cc, :],
                                    ubf_[:, g*512:(g+1)*512],
                                    start=(cc == 0), stop=(cc == 15))
                return last
            p.emit("tensor", mkbc, reads=[("wbc",), ("ubf", ub)],
                   writes=[("psbc",)], tag=f"bc{cc}")
            for c in range(16):
                tb = c % 2
                p.emit("tensor", lambda t, c=c, tb=tb, ubf_=ubf_: t.transpose(
                    mx.ps_tp[:, tb, 0:128], ubf_[:, c*128:(c+1)*128], p.idbf[:]),
                    reads=[("ubf", ub), ("idbf",)], writes=[("pstp", tb)],
                    tag="utp")
                p.emit("scalar", lambda s, c=c, tb=tb: s.copy(
                    mx.utc[:, ub, c, :], mx.ps_tp[:, tb, 0:128]),
                    reads=[("pstp", tb)], writes=[("utcs", ub)], tag="utce")
            sto_dma(utc_d[cc, t0:t0+SG, :].rearrange("(c p) n -> p c n", p=128),
                    mx.utc[:, ub, :, :],
                    reads=[("utcs", ub)], writes=[("utcst", b, cc)], tag="utco")

    Bm, Cm, Ssc = stash["Bm"], stash["Cm"], stash["S"]
    dB = stash["dB"]
    p.emit("vector", lambda v: v.tensor_copy(Bm[:], mx.ps_bc[0:16, :]),
           reads=[("psbc",)], writes=[("stash", b, "p1")])
    p.emit("vector", lambda v: v.tensor_copy(Cm[:], mx.ps_bc[32:48, :]),
           reads=[("psbc",)], writes=[("stash", b, "p1")])
    p.emit("scalar", lambda s: s.activation(mx.dpre[:], mx.ps_bc[64:80, :],
                                            AF.Identity, bias=p.bdt[:]),
           reads=[("psbc",), ("bdt",)], writes=[("dpre",)])
    p.emit("scalar", lambda s: s.activation(mx.da[:], mx.dpre[:], AF.Abs),
           reads=[("dpre",)], writes=[("da",)])
    p.emit("scalar", lambda s: s.activation(mx.da[:], mx.da[:], AF.Exp, scale=-1.0),
           reads=[("da",)], writes=[("da",)])
    p.emit("scalar", lambda s: s.activation(mx.da[:], mx.da[:], AF.Ln, bias=1.0),
           reads=[("da",)], writes=[("da",)])
    p.emit("vector", lambda v: v.tensor_relu(mx.db[:], mx.dpre[:]),
           reads=[("dpre",)], writes=[("db",), ("dpre",)])
    p.emit("vector", lambda v: v.tensor_add(mx.db[:], mx.db[:], mx.da[:]),
           reads=[("db",), ("da",)], writes=[("db",)])
    p.emit("vector", lambda v: v.tensor_mul(dB[:], mx.db[:], Bm[:]),
           reads=[("db",), ("stash", b, "p1")], writes=[("stash", b, "p2")])
    p.emit("vector", lambda v: v.tensor_scalar_mul(mx.da[:], mx.db[:],
                                                   p.acol[:]),
           reads=[("db",), ("acol",)], writes=[("da",)])
    p.emit("vector", lambda v: v.tensor_tensor_scan(
        Ssc[:], mx.da[:], mx.da[:], 0.0, ALU.add, ALU.bypass),
        reads=[("da",)], writes=[("stash", b, "p2")], tag="scan")


def _bar_and_flush(p):
    for e in ENGINES:
        p.wait_all_current(e)
    p.build_block()


def build_program():
    nc = bass.Bass("TRN2", target_bir_lowering=False)
    dr = lambda n, s, dt, k: nc.dram_tensor(n, s, dt, kind=k).ap()
    x_d = dr("x", [NTOK, D], F32, "ExternalInput")
    wq_d = dr("wq", [128, 8192], F32, "ExternalInput")
    wk_d = dr("wk", [128, 8192], F32, "ExternalInput")
    wv_d = dr("wv", [128, 8, 1024], F32, "ExternalInput")
    wo_d = dr("wo", [128, 8, 1024], F32, "ExternalInput")
    gma_d = dr("gma", [128, NH, 2, 128], F32, "ExternalInput")
    gnw_d = dr("gnw", [64, 16], F32, "ExternalInput")
    gnb_d = dr("gnb", [64, 16], F32, "ExternalInput")
    wffn_d = dr("wffn", [NE, NPIECE, 128, 16384], F32, "ExternalInput")
    cent_d = dr("centT", [D, 8], F32, "ExternalInput")
    rb_d = dr("rbias", [128, 8], F32, "ExternalInput")
    win_d = dr("win", [4, 128, 8192], F32, "ExternalInput")
    wbc_d = dr("wbc", [128, 16, 96], F32, "ExternalInput")
    cw_d = dr("cw", [128, 16, 4], F32, "ExternalInput")
    cb_d = dr("cb", [128, 16], F32, "ExternalInput")
    dd_d = dr("dd", [128, 16], F32, "ExternalInput")
    bdt_d = dr("bdt", [16, 1], F32, "ExternalInput")
    alog_d = dr("alog", [16, 1], F32, "ExternalInput")
    wout_d = dr("wout", [128, 16, 1024], F32, "ExternalInput")
    oneh_d = dr("oneh", [16, 16, 128], F32, "ExternalInput")
    kmask_d = dr("kmask", [128, 128], F32, "ExternalInput")
    idf_d = dr("idf", [128, 128], F32, "ExternalInput")
    out_d = dr("out", [NTOK, D], F32, "ExternalOutput")

    x1_d = dr("x1s", [NTOK, D], F32, "Internal")
    x2_d = dr("x2s", [NTOK, D], F32, "Internal")
    x3_d = dr("x3s", [NTOK, D], F32, "Internal")
    qT_d = dr("qTs", [D, NTOK], BF16, "Internal")
    kT_d = dr("kTs", [D, NTOK], BF16, "Internal")
    v_d = dr("vSs", [NTOK, D], BF16, "Internal")
    utc_d = dr("utcS", [16, NTOK, 128], BF16, "Internal")
    uD_d = dr("uDS", [DIN, NTOK], BF16, "Internal")
    sz_d = dr("szS", [DIN, NTOK], BF16, "Internal")

    gam_pows = [1.0 - 2.0 ** (-5.0 - i) for i in range(NH)]

    es = contextlib.ExitStack()
    with es:
        p = Prog(nc, es)
        cc = ConstCols(p, es, {"eps": EPS, "gneps": GN_EPS})
        p.idbf = sbuf(nc, es, "idbf", [128, 128], BF16)[:]
        p.bdt = sbuf(nc, es, "bdtT", [16, 1], F32)[:]
        p.acol = sbuf(nc, es, "acol", [16, 1], F32)[:]
        atmp = sbuf(nc, es, "atmp", [16, 1], F32)[:]
        p.dma("gpsimd", "c", p.idbf, idf_d, writes=[("idbf",)])
        p.dma("gpsimd", "c", p.bdt, bdt_d, writes=[("bdt",)])
        p.dma("gpsimd", "c", atmp, alog_d, writes=[("alog",)])
        p.emit("scalar", lambda s: s.activation(atmp, atmp, AF.Exp),
               reads=[("alog",)], writes=[("alog",)])
        p.emit("scalar", lambda s: s.mul(p.acol, atmp, -1.0),
               reads=[("alog",)], writes=[("acol",)])

        with contextlib.ExitStack() as bs:
            lx = LnCtx(p, bs, "a1")
            ax = AttnProjCtx(p, bs)
            ax.qT, ax.kT, ax.vS = qT_d, kT_d, v_d
            for j in range(4):
                sl = slice(j * 2048, (j + 1) * 2048)
                p.dma("gpsimd", "c", ax.wq[:, sl], wq_d[:, sl],
                      writes=[("wh", 0, j)])
            for j in range(4):
                sl = slice(j * 2048, (j + 1) * 2048)
                p.dma("gpsimd", "c", ax.wk[:, sl], wk_d[:, sl],
                      writes=[("wh", 1, j)])
            p.dma("gpsimd", "c", ax.wv[:], wv_d, writes=[("wh", 2, 0)])
            for sg in range(2):
                emit_ln_sg(p, lx, cc, x_d, sg, "xin")
                emit_attn_a_proj(p, lx, ax, sg)
            _bar_and_flush(p)

        with contextlib.ExitStack() as bs:
            rx = RetCtx(p, bs)
            rx.cc = cc
            p.dma("gpsimd", "c", rx.gma[:], gma_d, writes=[("gma",)])
            p.dma("gpsimd", "c", rx.gnw[:], gnw_d, writes=[("gnw",)])
            p.dma("gpsimd", "c", rx.gnb[:], gnb_d, writes=[("gnb",)])
            p.emit("vector", lambda v: v.memset(rx.ones64[:], 1.0),
                   writes=[("ones64",)])
            p.emit("vector", lambda v: v.memset(rx.onesr[:], 1.0),
                   writes=[("onesr",)])
            p.dma("gpsimd", "c", rx.wo[:], wo_d, writes=[("wo",)])
            for b in range(2):
                emit_retention_batch(p, rx, b, qT_d, kT_d, v_d, x_d, x1_d,
                                     "xin", "x1", gam_pows)
            _bar_and_flush(p)

        with contextlib.ExitStack() as bs:
            fx = FfnCtx(p, bs)
            fx.cc = cc
            fx.wffn_ap = wffn_d
            fx.load_consts(p, cent_d, rb_d)
            emit_ffn_call(p, fx, x1_d, x2_d, 0, "x1", "x2")
            _bar_and_flush(p)

        with contextlib.ExitStack() as mams:
            packs = []
            for b in range(2):
                packs.append({
                    "utc": utc_d, "uD": uD_d, "sz": sz_d,
                    "Cm": sbuf(nc, mams, f"stCm_{b}", [16, 2048], F32)[:],
                    "S": sbuf(nc, mams, f"stS_{b}", [16, 2048], F32)[:],
                    "dB": sbuf(nc, mams, f"stdB_{b}", [16, 2048], F32)[:],
                })
            with contextlib.ExitStack() as bs:
                mx = MbCtx(p, bs)
                lx = LnCtx(p, bs, "m1", stream=True, tp=mx.ps_tp, tpkey="pstp")
                mx.cc = cc
                mx.win_d = win_d
                bm_scr = sbuf(nc, bs, "stBm", [16, 2048], F32)[:]
                for pk in packs:
                    pk["Bm"] = bm_scr
                p.dma("gpsimd", "c", mx.wbc[:], wbc_d, writes=[("wbc",)])
                p.dma("gpsimd", "c", mx.cw[:], cw_d, writes=[("cw",)])
                p.dma("gpsimd", "c", mx.cb[:], cb_d, writes=[("cb",)])
                p.dma("gpsimd", "c", mx.dd[:], dd_d, writes=[("dd",)])
                p.emit("vector", lambda v: v.memset(mx.xi[:, 0, 0:3], 0.0),
                       writes=[("xi", 0)])
                p.emit("vector", lambda v: v.memset(mx.xi[:, 1, 0:3], 0.0),
                       writes=[("xi", 1)])
                for b in range(2):
                    emit_mamba_b2(p, lx, mx, b, x2_d, "x2", packs[b])
                    _bar_and_flush(p)
            with contextlib.ExitStack() as bs:
                sh = ScanShared(p, bs)
                p.dma("gpsimd", "c", sh.wout[:], wout_d, writes=[("wout",)])
                p.dma("gpsimd", "c", sh.oneh[:], oneh_d, writes=[("oneh",)])
                p.dma("gpsimd", "c", sh.kmask[:], kmask_d, writes=[("kmask",)])
                p.dma("gpsimd", "c", sh.idf[:], idf_d, writes=[("idf",)])
                p.emit("vector", lambda v: v.memset(sh.zc16[:], 0.0),
                       writes=[("zc16",)])
                sbs = [ScanBatch(p, bs, sh, b, packs[b], x2_d, x3_d, "x2", "x3")
                       for b in range(2)]
                for b in range(2):
                    sbs[b].init_emit()
                for c in range(16):
                    for b in range(2):
                        sbs[b].prep_emit(c)
                    for b in range(2):
                        sbs[b].rest_emit(c)
                _bar_and_flush(p)

        with contextlib.ExitStack() as bs:
            fx = FfnCtx(p, bs)
            fx.cc = cc
            fx.wffn_ap = wffn_d
            fx.load_consts(p, cent_d, rb_d)
            emit_ffn_call(p, fx, x3_d, out_d, 1, "x3", "xo")
            _bar_and_flush(p)
    return nc


def _host_pack(inp):
    Wq = inp["L0_attn_a_W_q"]; Wk = inp["L0_attn_a_W_k"]
    Wv = inp["L0_attn_a_W_v"]; Wo = inp["L0_attn_a_W_o"]
    oneh = np.zeros((16, 16, 128), np.float32)
    for n in range(16):
        oneh[n, n, :] = 1.0
    kmask = (np.arange(128)[None, :] >= np.arange(128)[:, None]).astype(np.float32)
    W_x = inp["L0_attn_b_W_x"]; W_dt = inp["L0_attn_b_W_dt"]
    wbc = np.concatenate([
        np.pad(np.ascontiguousarray(W_x.T[:, 0:16]), ((0, 0), (0, 16))),
        np.pad(np.ascontiguousarray(W_x.T[:, 16:32]), ((0, 0), (0, 16))),
        np.pad(np.ascontiguousarray(W_dt.T), ((0, 0), (0, 16)))], axis=1)
    shared = {
        "wq": pack_lhsT_blob(Wq),
        "wk": pack_lhsT_blob(Wk),
        "wv": pack_rhs_blob(Wv.T),
        "wo": pack_rhs_blob(Wo.T),
        "gma": np.ascontiguousarray(make_gamma_tiles().transpose(2, 0, 1, 3)),
        "gnw": np.ascontiguousarray(inp["L0_attn_a_gn_weight"].reshape(16, 64).T),
        "gnb": np.ascontiguousarray(inp["L0_attn_a_gn_bias"].reshape(16, 64).T),
        "wffn": pack_ffn_weights(inp["L0_ffn_Wfc_r"], inp["L0_ffn_Wproj_r"],
                                 inp["L0_ffn_Wfc_sh"], inp["L0_ffn_Wproj_sh"]),
        "centT": np.ascontiguousarray(inp["L0_ffn_centroids"].T),
        "rbias": np.ascontiguousarray(
            np.broadcast_to(inp["L0_ffn_routing_bias"], (128, 8))),
        "win": pack_fc_blob(inp["L0_attn_b_W_in"], 4),
        "wbc": np.ascontiguousarray(wbc.reshape(16, 128, 96).transpose(1, 0, 2)),
        "cw": np.ascontiguousarray(
            inp["L0_attn_b_conv_w"][:, 0, :].reshape(16, 128, 4).transpose(1, 0, 2)),
        "cb": np.ascontiguousarray(inp["L0_attn_b_conv_b"].reshape(16, 128).T),
        "dd": np.ascontiguousarray(inp["L0_attn_b_D"].reshape(16, 128).T),
        "bdt": np.ascontiguousarray(inp["L0_attn_b_b_dt"].reshape(16, 1)),
        "alog": np.ascontiguousarray(inp["L0_attn_b_A_log"].reshape(16, 1)),
        "wout": pack_rhs_blob(inp["L0_attn_b_W_out"].T),
        "oneh": oneh, "kmask": kmask,
        "idf": np.eye(128, dtype=np.float32),
    }
    shared = {k: np.ascontiguousarray(v, dtype=np.float32)
              for k, v in shared.items()}
    return shared


def kernel(**inputs):
    x_full = np.asarray(inputs["L0_attn_a_x"], dtype=np.float32)
    shared = _host_pack(inputs)
    nc = build_program()
    in_maps = []
    for c in range(8):
        m = dict(shared)
        m["x"] = np.ascontiguousarray(
            x_full[2*c:2*c+2].reshape(NTOK, D))
        in_maps.append(m)
    res = run_bass_kernel_spmd(nc, in_maps, core_ids=list(range(8)))
    out = np.empty((16, 2048, 1024), np.float32)
    for c in range(8):
        out[2*c:2*c+2] = res.results[c]["out"].reshape(2, 2048, 1024)
    return out


# version 15
# speedup vs baseline: 1.0818x; 1.0818x over previous
import numpy as np
import contextlib
import concourse.bass as bass
import concourse.mybir as mybir
from concourse.bass_utils import run_bass_kernel_spmd

F32 = mybir.dt.float32
BF16 = mybir.dt.bfloat16
AF = mybir.ActivationFunctionType
ALU = mybir.AluOpType
AX = mybir.AxisListType

ENGINES = ("tensor", "vector", "scalar", "gpsimd", "sync")

D = 1024
FF = 4096
NTOK = 4096
SG = 2048
NCHUNK = 16
NE = 10
NPIECE = 4
NH = 16
DH = 64
DIN = 2048
NST = 16
EPS = 1e-5
GN_EPS = 1e-5

class Prog:
    def __init__(self, nc, es):
        self.nc = nc
        self.es = es
        self.items = {e: [] for e in ENGINES}
        self.prog_sem = {}
        self.prog_cnt = {e: 0 for e in ENGINES}
        for e in ENGINES:
            self.prog_sem[e] = es.enter_context(nc.semaphore(f"prog_{e}"))
        self.dma_sem = {}
        self.dma_cnt = {}
        self.last_write = {}
        self.readers = {}
        self.waited = {e: {} for e in ENGINES}
        self.sem_handles = {}
        for e in ENGINES:
            self.sem_handles["prog_" + e] = self.prog_sem[e]
        self.scheduled_total = {("prog_" + e): 0 for e in ENGINES}
        self.final_waits = []

    DMA_WAYS = 4

    def _stream_sems(self, stream):
        if stream not in self.dma_sem:
            hs = []
            for j in range(self.DMA_WAYS):
                name = f"dma_{stream}_{j}"
                h = self.es.enter_context(self.nc.semaphore(name))
                hs.append(h)
                self.sem_handles[name] = h
                self.scheduled_total[name] = 0
            self.dma_sem[stream] = hs
            self.dma_cnt[stream] = 0
        return self.dma_sem[stream]

    def _gather_waits(self, engine, reads, writes):
        evs = []
        for r in reads:
            lw = self.last_write.get(r)
            if lw:
                evs.append(lw)
        for w in writes:
            lw = self.last_write.get(w)
            if lw:
                evs.append(lw)
            evs.extend(self.readers.get(w, ()))
        best = {}
        for semname, val in evs:
            if best.get(semname, -1) < val:
                best[semname] = val
        waits = []
        for semname, val in best.items():
            if engine == "tensor" and semname == "prog_tensor":
                continue
            if self.waited[engine].get(semname, -1) >= val:
                continue
            assert val <= self.scheduled_total[semname], (
                f"wait {semname}>={val} exceeds scheduled {self.scheduled_total[semname]}")
            waits.append((semname, val))
            self.waited[engine][semname] = val
        return waits

    def _finish(self, engine, completion, reads, writes):
        for w in writes:
            self.last_write[w] = completion
            self.readers[w] = []
        for r in reads:
            self.readers.setdefault(r, []).append(completion)

    def emit(self, engine, fn, reads=(), writes=(), tag="", n_inc=1):
        reads = list(reads); writes = list(writes)
        waits = self._gather_waits(engine, reads, writes)
        semname = "prog_" + engine
        self.prog_cnt[engine] += 1
        cnt = self.prog_cnt[engine]
        self.scheduled_total[semname] = cnt
        self.items[engine].append(("op", waits, fn, self.prog_sem[engine], 1, tag))
        completion = (semname, cnt)
        self._finish(engine, completion, reads, writes)
        return completion

    def dma(self, qengine, stream, out, in_, reads=(), writes=(), tag="", **kw):
        sems = self._stream_sems(stream)
        n = self.dma_cnt[stream]
        j = n % self.DMA_WAYS
        semname = f"dma_{stream}_{j}"
        sem = sems[j]
        reads = list(reads); writes = list(writes)
        waits = self._gather_waits(qengine, reads, writes)
        prev = 16 * (n // self.DMA_WAYS)
        if prev > 0 and self.waited[qengine].get(semname, -1) < prev:
            waits.append((semname, prev))
            self.waited[qengine][semname] = prev
        self.dma_cnt[stream] = n + 1
        val = 16 * (n // self.DMA_WAYS + 1)
        self.scheduled_total[semname] = val

        def fn(eng, out=out, in_=in_, kw=kw):
            return eng.dma_start(out=out, in_=in_, **kw)

        self.items[qengine].append(("op", waits, fn, sem, 16, tag))
        completion = (semname, val)
        self._finish(qengine, completion, reads, writes)
        return completion

    def barrier_wait(self, engine, events):
        waits = []
        for semname, val in events:
            if self.waited[engine].get(semname, -1) >= val:
                continue
            assert val <= self.scheduled_total[semname]
            waits.append((semname, val))
            self.waited[engine][semname] = val
        if waits:
            self.items[engine].append(("waits", waits, None, None, 0, "barrier"))

    def wait_all_current(self, engine):
        evs = [(s, v) for s, v in self.scheduled_total.items() if v > 0]
        self.barrier_wait(engine, evs)

    def build_block(self):
        nc = self.nc
        with nc.Block() as block:
            for ename in ENGINES:
                if not self.items[ename]:
                    continue
                def make(ename=ename):
                    def body(eng):
                        for kind, waits, fn, sem, inc, tag in self.items[ename]:
                            for semname, val in waits:
                                eng.wait_ge(self.sem_handles[semname], val)
                            if fn is not None:
                                inst = fn(eng)
                                if sem is not None:
                                    if inc == 1:
                                        inst.then_inc(sem, 1)
                                    else:
                                        inst.then_inc(sem, inc)
                    return body
                getattr(block, ename)(make())
        self.items = {e: [] for e in ENGINES}


_NAME_CTR = [0]


def _uname(name):
    _NAME_CTR[0] += 1
    return f"{name}__{_NAME_CTR[0]}"


def sbuf(nc, es, name, shape, dtype):
    return es.enter_context(nc.sbuf_tensor(_uname(name), shape, dtype))


class ConstCols:
    def __init__(self, p, es, values):
        self.t = sbuf(p.nc, es, "constcols", [128, max(1, len(values))], F32)
        self.idx = {}
        for j, (k, v) in enumerate(values.items()):
            self.idx[k] = j
            p.emit("vector", lambda e, j=j, v=v: e.memset(self.t[:, j:j+1], v),
                   writes=[("cc", k)], tag=f"const_{k}")

    def col(self, k):
        return self.t[:, self.idx[k]:self.idx[k]+1]

    def key(self, k):
        return ("cc", k)


def psum(nc, es, name, shape, dtype=F32):
    return es.enter_context(nc.psum_tensor(_uname(name), shape, dtype))





def pack_lhsT_blob(W):
    A = np.asarray(W)
    arr = A.reshape(1, 8, 128, 8, 128).transpose(0, 4, 1, 3, 2)
    return np.ascontiguousarray(arr.reshape(128, 8192))


def pack_rhs_blob(WT):
    A = np.asarray(WT)
    nk = A.shape[0] // 128
    return np.ascontiguousarray(A.reshape(nk, 128, A.shape[1]).transpose(1, 0, 2))


def make_gamma_wide():
    g = np.asarray([1.0 - 2.0 ** (-5.0 - i) for i in range(NH)], dtype=np.float64)
    e = np.arange(512.0)[None, :] - np.arange(128.0)[:, None]
    out = g[:, None, None] ** e[None, :, :]
    return np.ascontiguousarray(out.transpose(1, 0, 2).astype(np.float32))


def make_gamma_tiles():
    g = np.asarray([1.0 - 2.0 ** (-5.0 - i) for i in range(NH)], dtype=np.float64)
    idx = np.arange(128, dtype=np.float64)
    diff = idx[None, :] - idx[:, None]
    out = np.zeros((NH, 2, 128, 128), np.float32)
    for h in range(NH):
        full = g[h] ** diff
        out[h, 0] = full.astype(np.float32)
        out[h, 1] = np.where(diff >= 0, full, 0.0).astype(np.float32)
    return out


class LnCtx:
    def __init__(self, p, es, pref, stream=False, tp=None, tpkey=None):
        nc = p.nc
        self.stream = stream
        nxb = 2 if stream else 16
        self.xb = sbuf(nc, es, pref + "_xb", [128, nxb, 1024], F32)
        self.xlnT = sbuf(nc, es, pref + "_xlnT", [128, 8, 2048], BF16)
        self.xln_tm = sbuf(nc, es, pref + "_xlntm", [128, 2, 1024], BF16)
        if tp is None:
            tp = psum(nc, es, pref + "_tp", [128, 2, 1024], BF16)
            tpkey = pref + "_tp"
        self.tp = tp
        self.tpkey = tpkey
        self.sq = sbuf(nc, es, pref + "_sq", [128, 16], F32)
        self.sm = sbuf(nc, es, pref + "_sm", [128, 16], F32)
        self.mn = sbuf(nc, es, pref + "_mn", [128, 16], F32)
        self.m2t = sbuf(nc, es, pref + "_m2t", [128, 16], F32)
        self.var = sbuf(nc, es, pref + "_var", [128, 16], F32)
        self.lv = sbuf(nc, es, pref + "_lv", [128, 16], F32)
        self.rstd = sbuf(nc, es, pref + "_rstd", [128, 16], F32)
        self.junk = sbuf(nc, es, pref + "_junk", [128, 1024], F32)


def emit_ln_sg(p, lx, cc, xsrc, sg, xsrc_key, keep_x=True):
    t0 = sg * SG
    stream = getattr(lx, "stream", False)
    for i in range(NCHUNK):
        ii = (i % 2) if stream else i
        rows = slice(t0 + i * 128, t0 + (i + 1) * 128)
        xi = lx.xb[:, ii, :]
        p.dma("sync", "lnx", xi, xsrc[rows, :],
              reads=[(xsrc_key, sg, i)], writes=[("xb", ii)], tag="lnx")
        p.emit("vector", lambda v, a=xi, i=i, ii=ii: v.scalar_tensor_tensor(
            lx.junk[:], a, 1.0, a, ALU.mult, ALU.mult, accum_out=lx.sq[:, i:i+1]),
            reads=[("xb", ii)], writes=[("sq",), ("junk",)], tag="sumsq")
        p.emit("vector", lambda v, a=xi, i=i, ii=ii: v.reduce_sum(
            lx.sm[:, i:i+1], a, axis=AX.X),
            reads=[("xb", ii)], writes=[("sm",)], tag="sum")
    p.emit("scalar", lambda s: s.mul(lx.mn[:], lx.sm[:], 1.0 / D),
           reads=[("sm",)], writes=[("mn",)])
    p.emit("scalar", lambda s: s.activation(lx.m2t[:], lx.mn[:], AF.Square),
           reads=[("mn",)], writes=[("m2t",)])
    p.emit("vector", lambda v: v.scalar_tensor_tensor(
        lx.var[:], lx.sq[:], 1.0 / D, lx.m2t[:], ALU.mult, ALU.subtract),
        reads=[("sq",), ("m2t",)], writes=[("var",)])
    p.emit("scalar", lambda s: s.activation(lx.lv[:], lx.var[:], AF.Ln,
                                            bias=cc.col("eps")),
           reads=[("var",), cc.key("eps")], writes=[("lv",)])
    p.emit("scalar", lambda s: s.activation(lx.rstd[:], lx.lv[:], AF.Exp, scale=-0.5),
           reads=[("lv",)], writes=[("rstd",)])
    for i in range(NCHUNK):
        b = i % 2
        ii = (i % 2) if stream else i
        if stream:
            rows = slice(t0 + i * 128, t0 + (i + 1) * 128)
            p.dma("sync", "lnx", lx.xb[:, ii, :], xsrc[rows, :],
                  reads=[(xsrc_key, sg, i)], writes=[("xb", ii)], tag="lnx2")
        xt = lx.xln_tm[:, b, :]
        p.emit("vector", lambda v, i=i, ii=ii, xt=xt: v.tensor_scalar(
            xt, lx.xb[:, ii, :], lx.mn[:, i:i+1], lx.rstd[:, i:i+1],
            ALU.subtract, ALU.mult),
            reads=[("xb", ii), ("mn",), ("rstd",)], writes=[("xlntm", b)], tag="ln")
        for kc in range(8):
            j = kc % 2
            p.emit("tensor", lambda t, kc=kc, j=j, xt=xt: t.transpose(
                lx.tp[:, j, 0:128], xt[:, kc*128:(kc+1)*128], p.idbf),
                reads=[("xlntm", b), ("idbf",)], writes=[(lx.tpkey, j)],
                tag="lnT")
            p.emit("scalar", lambda s, kc=kc, j=j, i=i: s.copy(
                lx.xlnT[:, kc, i*128:(i+1)*128], lx.tp[:, j, 0:128]),
                reads=[(lx.tpkey, j)], writes=[("xlnT",)], tag="lnTe")


def emit_attn_a_proj(p, lx, ax, sg):
    nc = p.nc
    t0 = sg * SG
    for wi, (wsb, stash) in enumerate(((ax.wq, ax.qT), (ax.wk, ax.kT))):
        wkeys = [("wh", wi, j) for j in range(4)]
        for m in range(8):
            for g in range(4):
                pp = (m * 4 + g) % 2
                def mk(t, wsb=wsb, m=m, g=g, pp=pp):
                    last = None
                    for kc in range(8):
                        last = t.matmul(
                            ax.ps_p[:, pp, :],
                            wsb[:, m*1024 + kc*128: m*1024 + (kc+1)*128],
                            lx.xlnT[:, kc, g*512:(g+1)*512],
                            start=(kc == 0), stop=(kc == 7))
                    return last
                p.emit("tensor", mk, reads=wkeys + [("xlnT",)],
                       writes=[("psp", pp)], tag=f"qk{wi}.{m}.{g}")
                eb = (m * 4 + g) % 2
                p.emit("scalar", lambda s, pp=pp, eb=eb: s.copy(
                    ax.stg[:, eb, :], ax.ps_p[:, pp, :]),
                    reads=[("psp", pp)], writes=[("stg", eb)], tag="pevac")
                p.dma("gpsimd", "stq", stash[m*128:(m+1)*128,
                                                 t0 + g*512: t0 + (g+1)*512],
                      ax.stg[:, eb, :],
                      reads=[("stg", eb)], writes=[(f"qk{wi}", sg, m, g)], tag="stash")
    for i in range(NCHUNK):
        for n in range(2):
            pp = (i * 2 + n) % 2
            def mkv(t, i=i, n=n, pp=pp):
                last = None
                for kc in range(8):
                    last = t.matmul(
                        ax.ps_p[:, pp, :],
                        lx.xlnT[:, kc, i*128:(i+1)*128],
                        ax.wv[:, kc, n*512:(n+1)*512],
                        start=(kc == 0), stop=(kc == 7))
                return last
            p.emit("tensor", mkv, reads=[("wh", 2, 0), ("xlnT",)],
                   writes=[("psp", pp)], tag=f"v{i}.{n}")
            eb = (i * 2 + n) % 2
            p.emit("scalar", lambda s, pp=pp, eb=eb: s.copy(
                ax.stg[:, eb, :], ax.ps_p[:, pp, :]),
                reads=[("psp", pp)], writes=[("stg", eb)], tag="pevac")
            p.dma("gpsimd", "stq", ax.vS[t0 + i*128: t0 + (i+1)*128,
                                          n*512:(n+1)*512],
                  ax.stg[:, eb, :],
                  reads=[("stg", eb)], writes=[("vst", sg, i, n)], tag="stash")


class AttnProjCtx:
    def __init__(self, p, es):
        nc = p.nc
        self.wq = sbuf(nc, es, "aw_q", [128, 8192], BF16)[:]
        self.wk = sbuf(nc, es, "aw_k", [128, 8192], BF16)[:]
        self.wv = sbuf(nc, es, "aw_v", [128, 8, 1024], BF16)
        self.stg = sbuf(nc, es, "a_stg", [128, 2, 512], BF16)
        self.ps_p = psum(nc, es, "a_psp", [128, 2, 512], F32)


class RetCtx:
    def __init__(self, p, es):
        nc = p.nc
        self.qTb = sbuf(nc, es, "r_qTb", [128, 8, 2048], BF16)
        self.kTb = sbuf(nc, es, "r_kTb", [128, 8, 2048], BF16)
        self.vb = sbuf(nc, es, "r_vb", [128, 16, 1024], BF16)
        self.gma = sbuf(nc, es, "r_gma", [128, 16, 128], BF16)
        self.w0 = sbuf(nc, es, "r_w0", [128, 16, 512], BF16)
        self.att = sbuf(nc, es, "r_att", [128, 2, 6, 512], BF16)
        self.ybh = sbuf(nc, es, "r_ybh", [64, 2, 2048], BF16)
        self.st2 = sbuf(nc, es, "r_st2", [64, 2], F32)
        self.red = sbuf(nc, es, "r_red", [1, 2], F32)
        self.sc = {k: sbuf(nc, es, f"r_sc_{k}", [1, 1], F32)[:]
                   for k in ("mean", "msq", "var", "lv", "rstd", "t1")}
        self.b64 = {k: sbuf(nc, es, f"r_b64_{k}", [64, 1], F32)[:]
                    for k in ("m", "rg")}
        self.gnw = sbuf(nc, es, "r_gnw", [64, 16], F32)
        self.gnb = sbuf(nc, es, "r_gnb", [64, 16], F32)
        self.ones64 = sbuf(nc, es, "r_ones64", [64, 1], F32)
        self.onesr = sbuf(nc, es, "r_onesr", [1, 64], F32)
        self.t1 = sbuf(nc, es, "r_t1", [64, 2048], BF16)
        self.ynT = sbuf(nc, es, "r_ynT", [128, 8, 2048], BF16)
        self.wo = sbuf(nc, es, "r_wo", [128, 8, 1024], BF16)
        self.xr = sbuf(nc, es, "r_xr", [128, 2, 1024], F32)
        self.x1 = sbuf(nc, es, "r_x1", [128, 2, 1024], F32)
        self.psq = psum(nc, es, "r_psq", [128, 5, 512], F32)
        self.psy = psum(nc, es, "r_psy", [64, 2, 512], F32)
        self.psg = psum(nc, es, "r_psg", [64, 4], F32)




def emit_retention_batch(p, rx, b, qT_d, kT_d, v_d, xsrc, xdst, xsrc_key, xdst_key,
                         gam_pows):
    t0 = b * SG
    p.dma("gpsimd", "rq", rx.qTb[:],
          qT_d.rearrange("(kc p) t -> p kc t", p=128)[:, :, t0:t0+SG],
          writes=[("qTb",)], tag="ldq")
    p.dma("gpsimd", "rk", rx.kTb[:],
          kT_d.rearrange("(kc p) t -> p kc t", p=128)[:, :, t0:t0+SG],
          writes=[("kTb",)], tag="ldk")
    p.dma("gpsimd", "rv", rx.vb[:],
          v_d[t0:t0+SG, :].rearrange("(i p) n -> p i n", p=128),
          writes=[("vb",)], tag="ldv")

    DELAY = 4

    def emit_y(hp, g, ncs, cs):
        sl = cs % 6
        n0 = max(0, cs - 4 * g) * 128
        for idx in range(2):
            h = 2 * hp + idx
            p.emit("tensor", lambda t, h=h, idx=idx, cs=cs, sl=sl, ncs=ncs, n0=n0:
                   t.matmul(rx.psy[0:64, idx, n0:512],
                            rx.vb[:, cs, h*64:(h+1)*64],
                            rx.att[:, idx, sl, n0:512],
                            start=(cs == 0), stop=(cs == ncs - 1)),
                   reads=[("vb",), ("att", idx, sl)], writes=[("psy", idx)],
                   tag=f"y{hp}.{idx}.{g}.{cs}")

    qkctr = 0
    for hp in range(8):
        for g in range(4):
            ncs = 4 * g + 4
            for cs in range(ncs):
                nz = max(0, cs - 4 * g)
                n0 = nz * 128
                banks = []
                for idx in range(2):
                    bank = qkctr % 5
                    qkctr += 1
                    banks.append(bank)
                    hb = idx * 64
                    p.emit("tensor", lambda t, hb=hb, hp=hp, cs=cs, g=g, bank=bank,
                           n0=n0:
                           t.matmul(rx.psq[:, bank, n0:512],
                                    rx.kTb[hb:hb+64, hp, cs*128:(cs+1)*128],
                                    rx.qTb[hb:hb+64, hp, g*512+n0:(g+1)*512],
                                    start=True, stop=True),
                           reads=[("kTb",), ("qTb",)], writes=[("psq", bank)],
                           tag=f"qk{hp}.{idx}.{g}.{cs}")
                sl = cs % 6
                for idx in range(2):
                    h = 2 * hp + idx
                    bank = banks[idx]
                    cconst = float(gam_pows[h] ** (128 * (4 * g - cs)) * 0.125)
                    if cs < 4 * g:
                        p.emit("vector", lambda v, idx=idx, sl=sl, bank=bank,
                               cconst=cconst, h=h: v.scalar_tensor_tensor(
                                   rx.att[:, idx, sl, :],
                                   rx.psq[:, bank, :], cconst,
                                   rx.w0[:, h, :], ALU.mult, ALU.mult),
                               reads=[("psq", bank), ("w0",)],
                               writes=[("att", idx, sl)], tag="decayw")
                    else:
                        p.emit("vector", lambda v, idx=idx, sl=sl, bank=bank,
                               h=h, n0=n0: v.scalar_tensor_tensor(
                                   rx.att[:, idx, sl, n0:n0+128],
                                   rx.psq[:, bank, n0:n0+128], 0.125,
                                   rx.gma[:, h, :], ALU.mult, ALU.mult),
                               reads=[("psq", bank), ("gma",)],
                               writes=[("att", idx, sl)], tag="decayd")
                        if nz < 3:
                            p.emit("vector", lambda v, idx=idx, sl=sl, bank=bank,
                                   cconst=cconst, h=h, n0=n0:
                                   v.scalar_tensor_tensor(
                                       rx.att[:, idx, sl, n0+128:512],
                                       rx.psq[:, bank, n0+128:512], cconst,
                                       rx.w0[:, h, n0+128:512], ALU.mult, ALU.mult),
                                   reads=[("psq", bank), ("w0",)],
                                   writes=[("att", idx, sl)], tag="decayt")
                if cs >= DELAY:
                    emit_y(hp, g, ncs, cs - DELAY)
            for cs in range(max(0, ncs - DELAY), ncs):
                emit_y(hp, g, ncs, cs)
            for idx in range(2):
                p.emit("scalar", lambda s, idx=idx, g=g: s.copy(
                    rx.ybh[:, idx, g*512:(g+1)*512], rx.psy[0:64, idx, :]),
                    reads=[("psy", idx)], writes=[("ybh", idx)], tag="yevac")
        for idx in range(2):
            emit_gn_head(p, rx, 2 * hp + idx, idx)

    emit_wo_residual(p, rx, b, xsrc, xdst, xsrc_key, xdst_key)


def emit_gn_head(p, rx, h, idx):
    hb = idx * 64
    hc = h // 2
    ybh = rx.ybh[:, idx, :]
    p.emit("vector", lambda v: v.reduce_sum(rx.st2[:, 0:1], ybh, axis=AX.X),
           reads=[("ybh", idx)], writes=[("st2",)])
    p.emit("vector", lambda v: v.scalar_tensor_tensor(
        rx.t1[:], ybh, 1.0, ybh, ALU.mult, ALU.mult,
        accum_out=rx.st2[:, 1:2]),
        reads=[("ybh", idx)], writes=[("st2",), ("t1",)])
    p.emit("tensor", lambda t: t.matmul(rx.psg[0:1, 0:2], rx.ones64[:],
                                        rx.st2[:], start=True, stop=True),
           reads=[("st2",), ("ones64",)], writes=[("psg",)], tag="gnred")
    p.emit("scalar", lambda s: s.copy(rx.red[:], rx.psg[0:1, 0:2]),
           reads=[("psg",)], writes=[("red",)])
    NT = 64 * 2048
    p.emit("scalar", lambda s: s.mul(rx.sc["mean"], rx.red[:, 0:1], 1.0 / NT),
           reads=[("red",)], writes=[("scmean",)])
    p.emit("scalar", lambda s: s.mul(rx.sc["msq"], rx.red[:, 1:2], 1.0 / NT),
           reads=[("red",)], writes=[("scmsq",)])
    p.emit("scalar", lambda s: s.activation(rx.sc["t1"], rx.sc["mean"], AF.Square),
           reads=[("scmean",)], writes=[("sct1",)])
    p.emit("vector", lambda v: v.tensor_sub(rx.sc["var"], rx.sc["msq"], rx.sc["t1"]),
           reads=[("scmsq",), ("sct1",)], writes=[("scvar",)])
    p.emit("scalar", lambda s: s.activation(rx.sc["lv"], rx.sc["var"], AF.Ln,
                                            bias=rx.cc.col("gneps")[0:1, :]),
           reads=[("scvar",), rx.cc.key("gneps")], writes=[("sclv",)])
    p.emit("scalar", lambda s: s.activation(rx.sc["rstd"], rx.sc["lv"], AF.Exp,
                                            scale=-0.5),
           reads=[("sclv",)], writes=[("scrstd",)])
    def mkb(t):
        t.matmul(rx.psg[0:64, 1:2], rx.onesr[:], rx.sc["mean"],
                 start=True, stop=True)
        return t.matmul(rx.psg[0:64, 2:3], rx.onesr[:], rx.sc["rstd"],
                        start=True, stop=True)
    p.emit("tensor", mkb, reads=[("scmean",), ("scrstd",), ("onesr",)],
           writes=[("psg",)], tag="gnbcast")
    p.emit("scalar", lambda s: s.copy(rx.b64["m"], rx.psg[0:64, 1:2]),
           reads=[("psg",)], writes=[("b64m",)])
    p.emit("vector", lambda v, h=h: v.tensor_scalar_mul(
        rx.b64["rg"], rx.psg[0:64, 2:3], rx.gnw[:, h:h+1]),
        reads=[("psg",), ("gnw",)], writes=[("b64rg",)])
    p.emit("vector", lambda v: v.tensor_scalar(
        rx.t1[:], ybh, rx.b64["m"], rx.b64["rg"], ALU.subtract, ALU.mult),
        reads=[("ybh", idx), ("b64m",), ("b64rg",)], writes=[("t1",)])
    p.emit("scalar", lambda s, hb=hb, hc=hc, h=h: s.activation(
        rx.ynT[hb:hb+64, hc, :], rx.t1[:], AF.Identity,
        bias=rx.gnb[:, h:h+1]),
        reads=[("t1",), ("gnb",)], writes=[("ynT", h)], tag="gnout")


def emit_wo_residual(p, rx, b, xsrc, xdst, xsrc_key, xdst_key):
    t0 = b * SG
    for i in range(NCHUNK):
        rows = slice(t0 + i * 128, t0 + (i + 1) * 128)
        xb = i % 2
        p.dma("sync", "lnx", rx.xr[:, xb, :], xsrc[rows, :],
              reads=[(xsrc_key, b, i)], writes=[("xr", xb)], tag="xres")
        for n in range(2):
            pp = (i * 2 + n) % 5
            def mko(t, i=i, n=n, pp=pp):
                last = None
                for kc in range(8):
                    last = t.matmul(rx.psq[:, pp, :],
                                    rx.ynT[:, kc, i*128:(i+1)*128],
                                    rx.wo[:, kc, n*512:(n+1)*512],
                                    start=(kc == 0), stop=(kc == 7))
                return last
            p.emit("tensor", mko,
                   reads=[("wo",)] + [("ynT", hh) for hh in range(16)],
                   writes=[("psq", pp)], tag=f"wo{i}.{n}")
            p.emit("vector", lambda v, i=i, n=n, pp=pp, xb=xb: v.tensor_add(
                rx.x1[:, xb, n*512:(n+1)*512], rx.psq[:, pp, :],
                rx.xr[:, xb, n*512:(n+1)*512]),
                reads=[("psq", pp), ("xr", xb)], writes=[("x1", xb)], tag="resadd")
        p.dma("sync", "xo", xdst[rows, :], rx.x1[:, xb, :],
              reads=[("x1", xb)], writes=[(xdst_key, b, i)], tag="x1out")







def pack_ffn_weights(Wfc_r, Wproj_r, Wfc_sh, Wproj_sh):
    fcs = [Wfc_r[e] for e in range(8)] + [Wfc_sh[0], Wfc_sh[1]]
    prs = [Wproj_r[e] for e in range(8)] + [Wproj_sh[0], Wproj_sh[1]]
    out = np.empty((NE, NPIECE, 128, 16384), np.float32)
    for e in range(NE):
        A = np.asarray(fcs[e])
        arr = A.reshape(4, 8, 128, 8, 128).transpose(0, 4, 1, 3, 2)
        out[e, :, :, 0:8192] = arr.reshape(4, 128, 8192)
        P = np.asarray(prs[e])
        arr2 = P.reshape(1024, 4, 8, 128).transpose(1, 3, 2, 0)
        out[e, :, :, 8192:16384] = arr2.reshape(4, 128, 8192)
    return out


class FfnCtx:
    def __init__(self, p, es):
        nc = p.nc
        self.acc = sbuf(nc, es, "ffn_acc", [128, 16, 1024], F32)
        self.xlnT = sbuf(nc, es, "ffn_xlnT", [128, 8, 2048], BF16)
        self.xln_tm = sbuf(nc, es, "ffn_xlntm", [128, 2, 1024], BF16)
        self.wp = sbuf(nc, es, "ffn_wp", [128, 2, 16384], BF16)
        self.h = sbuf(nc, es, "ffn_h", [128, 8, 512], BF16)
        self.sq = sbuf(nc, es, "ffn_sq", [128, 16], F32)
        self.sm = sbuf(nc, es, "ffn_sm", [128, 16], F32)
        self.mn = sbuf(nc, es, "ffn_mn", [128, 16], F32)
        self.m2t = sbuf(nc, es, "ffn_m2t", [128, 16], F32)
        self.var = sbuf(nc, es, "ffn_var", [128, 16], F32)
        self.lv = sbuf(nc, es, "ffn_lv", [128, 16], F32)
        self.rstd = sbuf(nc, es, "ffn_rstd", [128, 16], F32)
        self.r1 = sbuf(nc, es, "ffn_r1", [128, 16], F32)
        self.mr = sbuf(nc, es, "ffn_mr", [128, 16], F32)
        self.junk = sbuf(nc, es, "ffn_junk", [128, 1024], F32)
        self.cmb = sbuf(nc, es, "ffn_cmb", [128, 16, 8], F32)
        self.rt = {k: sbuf(nc, es, f"ffn_rt_{k}", [128, 8], F32)[:]
                   for k in ("balin", "bal", "sco", "bal2", "mask1", "mask2", "j8")}
        self.rc = {k: sbuf(nc, es, f"ffn_rc_{k}", [128, 1], F32)[:]
                   for k in ("m1", "m2", "s1", "s2", "wa", "wb", "neg")}
        self.rbias = sbuf(nc, es, "ffn_rbias", [128, 8], F32)
        self.centT = sbuf(nc, es, "ffn_centT", [128, 8, 8], BF16)
        self.ps_h = psum(nc, es, "ffn_ps_h", [128, 2, 512], F32)
        self.ps_o = psum(nc, es, "ffn_ps_o", [128, 3, 512], F32)
        self.ps_r = psum(nc, es, "ffn_ps_r", [128, 8], F32)
        self.ps_t = psum(nc, es, "ffn_ps_t", [128, 2, 1024], BF16)

    def load_consts(self, p, centT_dram, rbias_dram):
        p.dma("gpsimd", "c", self.centT[:],
              centT_dram.rearrange("(kc p) e -> p kc e", p=128),
              writes=[("centT",)], tag="centT")
        p.dma("gpsimd", "c", self.rbias[:], rbias_dram, writes=[("rbias",)], tag="rbias")


def emit_ffn_call(p, fx, xsrc, xdst, call_id, xsrc_key, xdst_key):
    nc = p.nc
    wffn = fx.wffn_ap
    pieces = [(e, pc) for e in range(NE) for pc in range(NPIECE)]

    for sg in range(2):
        t0 = sg * SG
        for i in range(NCHUNK):
            rows = slice(t0 + i * 128, t0 + (i + 1) * 128)
            acc_i = fx.acc[:, i, :]
            p.dma("sync", "accin", acc_i, xsrc[rows, :],
                  reads=[(xsrc_key, sg, i)], writes=[("acc", i)], tag=f"accin{call_id}")
            p.emit("vector", lambda v, a=acc_i, i=i: v.scalar_tensor_tensor(
                fx.junk[:], a, 1.0, a, ALU.mult, ALU.mult, accum_out=fx.sq[:, i:i+1]),
                reads=[("acc", i)], writes=[("sq",), ("junk",)], tag="sumsq")
            p.emit("vector", lambda v, a=acc_i, i=i: v.reduce_sum(
                fx.sm[:, i:i+1], a, axis=AX.X),
                reads=[("acc", i)], writes=[("sm",)], tag="sum")
        p.emit("scalar", lambda s: s.mul(fx.mn[:], fx.sm[:], 1.0 / D),
               reads=[("sm",)], writes=[("mn",)])
        p.emit("scalar", lambda s: s.activation(fx.m2t[:], fx.mn[:], AF.Square),
               reads=[("mn",)], writes=[("m2t",)])
        p.emit("vector", lambda v: v.scalar_tensor_tensor(
            fx.var[:], fx.sq[:], 1.0 / D, fx.m2t[:], ALU.mult, ALU.subtract),
            reads=[("sq",), ("m2t",)], writes=[("var",)])
        p.emit("scalar", lambda s: s.activation(fx.lv[:], fx.var[:], AF.Ln,
                                                bias=fx.cc.col("eps")),
               reads=[("var",), fx.cc.key("eps")], writes=[("lv",)])
        p.emit("scalar", lambda s: s.activation(fx.rstd[:], fx.lv[:], AF.Exp, scale=-0.5),
               reads=[("lv",)], writes=[("rstd",)])
        p.emit("scalar", lambda s: s.add(fx.r1[:], fx.rstd[:], 1.0),
               reads=[("rstd",)], writes=[("r1",)])
        p.emit("vector", lambda v: v.tensor_mul(fx.mr[:], fx.mn[:], fx.rstd[:]),
               reads=[("mn",), ("rstd",)], writes=[("mr",)])

        for i in range(NCHUNK):
            b = i % 2
            xt = fx.xln_tm[:, b, :]
            p.emit("vector", lambda v, i=i, xt=xt: v.tensor_scalar(
                xt, fx.acc[:, i, :], fx.mn[:, i:i+1], fx.rstd[:, i:i+1],
                ALU.subtract, ALU.mult),
                reads=[("acc", i), ("mn",), ("rstd",)], writes=[("xlntm", b)], tag="ln")
            p.emit("vector", lambda v, i=i: v.tensor_scalar(
                fx.acc[:, i, :], fx.acc[:, i, :], fx.r1[:, i:i+1], fx.mr[:, i:i+1],
                ALU.mult, ALU.subtract),
                reads=[("acc", i), ("r1",), ("mr",)], writes=[("acc", i)], tag="lnres")
            for kc in range(8):
                j = kc % 2
                p.emit("tensor", lambda t, kc=kc, j=j, xt=xt: t.transpose(
                    fx.ps_t[:, j, 0:128], xt[:, kc*128:(kc+1)*128], p.idbf),
                    reads=[("xlntm", b), ("idbf",)], writes=[("pst", j)],
                    tag="lnT")
                p.emit("scalar", lambda s, kc=kc, j=j, i=i: s.copy(
                    fx.xlnT[:, kc, i*128:(i+1)*128], fx.ps_t[:, j, 0:128]),
                    reads=[("pst", j)], writes=[("xlnT",)], tag="lnTe")
            def mk_raw(t, i=i):
                last = None
                for kc in range(8):
                    last = t.matmul(fx.ps_r[:], fx.xlnT[:, kc, i*128:(i+1)*128],
                                    fx.centT[:, kc, :], start=(kc == 0), stop=(kc == 7))
                return last
            p.emit("tensor", mk_raw, reads=[("xlnT",), ("centT",)], writes=[("psr",)],
                   tag="rawmm")
            rt, rc = fx.rt, fx.rc
            p.emit("vector", lambda v: v.tensor_add(rt["balin"], fx.ps_r[:], fx.rbias[:]),
                   reads=[("psr",), ("rbias",)], writes=[("balin",)])
            p.emit("scalar", lambda s: s.activation(rt["bal"], rt["balin"], AF.Sigmoid),
                   reads=[("balin",)], writes=[("bal",)])
            p.emit("scalar", lambda s: s.activation(rt["sco"], fx.ps_r[:], AF.Sigmoid),
                   reads=[("psr",)], writes=[("sco",)])
            p.emit("vector", lambda v: v.reduce_max(rc["m1"], rt["bal"], axis=AX.X),
                   reads=[("bal",)], writes=[("m1",)])
            p.emit("vector", lambda v: v.tensor_scalar(
                rt["mask1"], rt["bal"], rc["m1"], None, ALU.is_ge),
                reads=[("bal",), ("m1",)], writes=[("mask1",)])
            p.emit("vector", lambda v: v.scalar_tensor_tensor(
                rt["bal2"], rt["mask1"], -1e30, rt["bal"], ALU.mult, ALU.add),
                reads=[("mask1",), ("bal",)], writes=[("bal2",)])
            p.emit("vector", lambda v: v.reduce_max(rc["m2"], rt["bal2"], axis=AX.X),
                   reads=[("bal2",)], writes=[("m2",)])
            p.emit("vector", lambda v: v.tensor_scalar(
                rt["mask2"], rt["bal2"], rc["m2"], None, ALU.is_ge),
                reads=[("bal2",), ("m2",)], writes=[("mask2",)])
            p.emit("vector", lambda v: v.scalar_tensor_tensor(
                rt["j8"], rt["sco"], 1.0, rt["mask1"], ALU.mult, ALU.mult,
                accum_out=rc["s1"]),
                reads=[("sco",), ("mask1",)], writes=[("s1",), ("j8",)])
            p.emit("vector", lambda v: v.scalar_tensor_tensor(
                rt["j8"], rt["sco"], 1.0, rt["mask2"], ALU.mult, ALU.mult,
                accum_out=rc["s2"]),
                reads=[("sco",), ("mask2",)], writes=[("s2",), ("j8",)])
            p.emit("scalar", lambda s: s.mul(rc["neg"], rc["s2"], -1.0),
                   reads=[("s2",)], writes=[("neg",)])
            p.emit("scalar", lambda s: s.activation(rc["wa"], rc["s1"], AF.Sigmoid,
                                                    bias=rc["neg"]),
                   reads=[("s1",), ("neg",)], writes=[("wa",)])
            p.emit("scalar", lambda s: s.mul(rc["neg"], rc["s1"], -1.0),
                   reads=[("s1",), ("wa",)], writes=[("neg",)])
            p.emit("scalar", lambda s: s.activation(rc["wb"], rc["s2"], AF.Sigmoid,
                                                    bias=rc["neg"]),
                   reads=[("s2",), ("neg",)], writes=[("wb",)])
            p.emit("vector", lambda v: v.tensor_scalar(
                rt["j8"], rt["mask1"], rc["wa"], None, ALU.mult),
                reads=[("mask1",), ("wa",)], writes=[("j8",)])
            p.emit("vector", lambda v, i=i: v.scalar_tensor_tensor(
                fx.cmb[:, i, :], rt["mask2"], rc["wb"], rt["j8"], ALU.mult, ALU.add),
                reads=[("mask2",), ("wb",), ("j8",)], writes=[("cmb", i)], tag="cmb")

        qctr = 0
        for pi, (e, pc) in enumerate(pieces):
            wbuf = pi % 2
            if pi == 0:
                p.dma("gpsimd", "wp", fx.wp[:, 0, :], wffn[e, pc, :, :],
                      reads=[], writes=[("wp", 0)], tag="wdma")
            if pi + 1 < len(pieces):
                e2, pc2 = pieces[pi + 1]
                p.dma("gpsimd", "wp", fx.wp[:, (pi + 1) % 2, :], wffn[e2, pc2, :, :],
                      reads=[], writes=[("wp", (pi + 1) % 2)], tag="wdma")
            for g in range(4):
                gs = slice(g * 512, (g + 1) * 512)
                for f in range(8):
                    pp = f % 2
                    def mk_fc(t, f=f, pp=pp, gs=gs, wbuf=wbuf):
                        last = None
                        for kc in range(8):
                            last = t.matmul(
                                fx.ps_h[:, pp, :],
                                fx.wp[:, wbuf, f*1024 + kc*128: f*1024 + (kc+1)*128],
                                fx.xlnT[:, kc, gs],
                                start=(kc == 0), stop=(kc == 7))
                        return last
                    p.emit("tensor", mk_fc,
                           reads=[("wp", wbuf), ("xlnT",)], writes=[("psh", pp)],
                           tag=f"fc{e}.{pc}.{g}.{f}")
                    p.emit("scalar", lambda s, f=f, pp=pp: s.activation(
                        fx.h[:, f, :], fx.ps_h[:, pp, :], AF.Gelu),
                        reads=[("psh", pp)], writes=[("hsb", f)], tag="gelu")
                for tsub in range(4):
                    for dh in range(2):
                        q = qctr % 3
                        qctr += 1
                        for half in range(2):
                            frange = range(half * 4, half * 4 + 4)
                            def mk_pj(t, frange=frange, tsub=tsub, dh=dh, q=q,
                                      wbuf=wbuf, half=half):
                                last = None
                                for f in frange:
                                    last = t.matmul(
                                        fx.ps_o[:, q, :],
                                        fx.h[:, f, tsub*128:(tsub+1)*128],
                                        fx.wp[:, wbuf,
                                              8192 + f*1024 + dh*512:
                                              8192 + f*1024 + (dh+1)*512],
                                        start=(half == 0 and f == frange[0]),
                                        stop=(half == 1 and f == frange[-1]))
                                return last
                            p.emit("tensor", mk_pj,
                                   reads=[("wp", wbuf)] + [("hsb", f) for f in frange],
                                   writes=[("pso", q)], tag=f"pj{e}.{pc}.{g}.{tsub}.{dh}")
                        i = g * 4 + tsub
                        accs = fx.acc[:, i, dh*512:(dh+1)*512]
                        if e < 8:
                            p.emit("vector", lambda v, q=q, accs=accs, i=i, e=e:
                                   v.scalar_tensor_tensor(
                                       accs, fx.ps_o[:, q, :], fx.cmb[:, i, e:e+1],
                                       accs, ALU.mult, ALU.add),
                                   reads=[("pso", q), ("cmb", i), ("acc", i)],
                                   writes=[("acc", i)], tag="evac")
                        else:
                            p.emit("vector", lambda v, q=q, accs=accs:
                                   v.tensor_add(accs, fx.ps_o[:, q, :], accs),
                                   reads=[("pso", q), ("acc", i)],
                                   writes=[("acc", i)], tag="evac")
        for i in range(NCHUNK):
            rows = slice(t0 + i * 128, t0 + (i + 1) * 128)
            p.dma("sync", "xo", xdst[rows, :], fx.acc[:, i, :],
                  reads=[("acc", i)], writes=[(xdst_key, sg, i)], tag="xout")







def pack_fc_blob(W, npiece):
    A = np.asarray(W)
    F = A.shape[0]
    arr = A.reshape(npiece, 8, 128, 8, 128).transpose(0, 4, 1, 3, 2)
    return np.ascontiguousarray(arr.reshape(npiece, 128, 8192))


class MbCtx:
    def __init__(self, p, es):
        nc = p.nc
        self.win = sbuf(nc, es, "m_win", [128, 2, 8192], BF16)
        self.wbc = sbuf(nc, es, "m_wbc", [128, 16, 96], BF16)
        self.cw = sbuf(nc, es, "m_cw", [128, 16, 4], F32)
        self.cb = sbuf(nc, es, "m_cb", [128, 16], F32)
        self.dd = sbuf(nc, es, "m_dd", [128, 16], F32)
        self.xi = sbuf(nc, es, "m_xi", [128, 2, 2052], BF16)
        self.u = sbuf(nc, es, "m_u", [128, 1, 2048], F32)
        self.c0 = self.u
        self.ubf = sbuf(nc, es, "m_ubf", [128, 1, 2048], BF16)
        self.uD = sbuf(nc, es, "m_uD", [128, 2048], F32)
        self.sz = sbuf(nc, es, "m_sz", [128, 2, 512], BF16)
        self.utc = sbuf(nc, es, "m_utc", [128, 1, 16, 128], BF16)
        self.dpre = sbuf(nc, es, "m_dpre", [16, 2048], F32)
        self.da = sbuf(nc, es, "m_da", [16, 2048], F32)
        self.db = self.dpre
        self.ps_xz = psum(nc, es, "m_psxz", [128, 2, 512], F32)
        self.ps_bc = psum(nc, es, "m_psbc", [96, 2048], F32)
        self.ps_tp = psum(nc, es, "m_pstp", [128, 2, 1024], BF16)


class ScanShared:
    def __init__(self, p, es):
        nc = p.nc
        self.wout = sbuf(nc, es, "s_wout", [128, 16, 1024], BF16)
        self.oneh = sbuf(nc, es, "s_oneh", [16, 16, 128], F32)
        self.kmask = sbuf(nc, es, "s_kmask", [128, 128], F32)
        self.idf = sbuf(nc, es, "s_idf", [128, 128], F32)
        self.zc16 = sbuf(nc, es, "s_zc16", [16, 1], F32)


class ScanBatch:
    def __init__(self, p, es, sh, b, stash, xsrc, xdst, xsrc_key, xdst_key):
        nc = p.nc
        self.p, self.sh, self.b = p, sh, b
        self.stash = stash
        self.xsrc, self.xdst = xsrc, xdst
        self.xsrc_key, self.xdst_key = xsrc_key, xdst_key
        pre = f"sb{b}"
        self.utc = sbuf(nc, es, pre + "_utc", [128, 2, 16, 128], BF16)
        self.uDc = sbuf(nc, es, pre + "_uDc", [128, 2, 16, 128], BF16)
        self.szc = sbuf(nc, es, pre + "_szc", [128, 16, 128], BF16)
        self.G = sbuf(nc, es, pre + "_G", [128, 2, 128], F32)
        self.t2 = sbuf(nc, es, pre + "_t2", [128, 2, 128], F32)
        self.kacc = sbuf(nc, es, pre + "_kacc", [128, 2, 128], F32)
        self.kbf = sbuf(nc, es, pre + "_kbf", [128, 128], BF16)
        self.dBT = sbuf(nc, es, pre + "_dBT", [128, 16], F32)
        self.ST = sbuf(nc, es, pre + "_ST", [128, 16], F32)
        self.negST = sbuf(nc, es, pre + "_negST", [128, 16], F32)
        self.E = sbuf(nc, es, pre + "_E", [16, 128], F32)
        self.EdB = sbuf(nc, es, pre + "_EdB", [16, 128], F32)
        self.EdBT = sbuf(nc, es, pre + "_EdBT", [128, 16], BF16)
        self.P = sbuf(nc, es, pre + "_P", [16, 128], F32)
        self.CP = sbuf(nc, es, pre + "_CP", [16, 128], BF16)
        self.ptot = sbuf(nc, es, pre + "_ptot", [16, 1], F32)
        self.t16 = sbuf(nc, es, pre + "_t16", [16, 128], F32)
        self.ys = sbuf(nc, es, pre + "_ys", [128, 16, 128], BF16)
        self.hst = sbuf(nc, es, pre + "_hst", [16, 2048], F32)
        self.hstbf = sbuf(nc, es, pre + "_hstbf", [16, 2048], BF16)
        self.xr = sbuf(nc, es, pre + "_xr", [128, 1024], F32)
        self.x3 = sbuf(nc, es, pre + "_x3", [128, 1024], F32)
        self.psA = psum(nc, es, pre + "_psA", [128, 2, 512], F32)
        self.psY = psum(nc, es, pre + "_psY", [128, 4, 128], F32)
        self.psW = psum(nc, es, pre + "_psW", [128, 512], F32)

    def K(self, *a):
        return ("s", self.b) + a

    def init_emit(self):
        p, K = self.p, self.K
        p.emit("vector", lambda v: v.memset(self.hst[:], 0.0),
               writes=[K("hst")], tag="hini")
        p.emit("vector", lambda v: v.memset(self.hstbf[:], 0.0),
               writes=[K("hstbf")], tag="hini2")

    def prep_emit(self, c):
        p, sh, b, K = self.p, self.sh, self.b, self.K
        sx = self
        cb = c % 2
        t0 = b * SG
        S, Cm, dB = self.stash["S"], self.stash["Cm"], self.stash["dB"]
        utc_d, uD_d, sz_d = self.stash["utc"], self.stash["uD"], self.stash["sz"]
        trange = slice(t0 + c*128, t0 + (c+1)*128)
        p.dma("gpsimd", f"sci{b}", sx.utc[:, cb, :, :],
              utc_d[:, trange, :].rearrange("cc t n -> t cc n"),
              writes=[K("utc", cb)], tag="utci")
        p.dma("gpsimd", f"sci{b}", sx.uDc[:, cb, :, :],
              uD_d[:, trange].rearrange("(cc p) t -> p cc t", p=128),
              writes=[K("uDc", cb)], tag="uDi")
        p.dma("gpsimd", f"scz{b}", sx.szc[:],
              sz_d[:, trange].rearrange("(cc p) t -> p cc t", p=128),
              writes=[K("szc")], tag="szi")
        Sc = S[:, c*128:(c+1)*128]
        Cc = Cm[:, c*128:(c+1)*128]
        dBc = dB[:, c*128:(c+1)*128]
        send = S[:, c*128+127: c*128+128]
        sprev = sh.zc16[:] if c == 0 else S[:, c*128-1: c*128]

        p.emit("tensor", lambda t, dBc=dBc: t.transpose(
            sx.psA[:, 0, 0:16], dBc, sh.idf[0:16, 0:16]),
            reads=[("stash", b, "p2"), ("idf",)], writes=[K("psA", 0)], tag="dBT")
        p.emit("vector", lambda v: v.tensor_copy(sx.dBT[:], sx.psA[:, 0, 0:16]),
               reads=[K("psA", 0)], writes=[K("dBT")])
        p.emit("tensor", lambda t, Sc=Sc: t.transpose(
            sx.psA[:, 1, 0:16], Sc, sh.idf[0:16, 0:16]),
            reads=[("stash", b, "p2"), ("idf",)], writes=[K("psA", 1)], tag="STt")
        p.emit("vector", lambda v: v.tensor_copy(sx.ST[:], sx.psA[:, 1, 0:16]),
               reads=[K("psA", 1)], writes=[K("ST")])
        p.emit("vector", lambda v, Sc=Sc, send=send: v.tensor_scalar(
            sx.E[:], Sc, send, -1.0, ALU.subtract, ALU.mult),
            reads=[("stash", b, "p2")], writes=[K("E")])
        p.emit("scalar", lambda s: s.activation(sx.E[:], sx.E[:], AF.Exp),
               reads=[K("E")], writes=[K("E")])
        p.emit("vector", lambda v, dBc=dBc: v.tensor_mul(sx.EdB[:], sx.E[:], dBc),
               reads=[K("E"), ("stash", b, "p2")], writes=[K("EdB")])
        p.emit("tensor", lambda t: t.transpose(
            sx.psA[:, 0, 0:16], sx.EdB[:], sh.idf[0:16, 0:16]),
            reads=[K("EdB"), ("idf",)], writes=[K("psA", 0)], tag="EdBT")
        p.emit("scalar", lambda s: s.copy(sx.EdBT[:], sx.psA[:, 0, 0:16]),
               reads=[K("psA", 0)], writes=[K("EdBT")])
        p.emit("vector", lambda v, Sc=Sc, sprev=sprev: v.tensor_scalar(
            sx.P[:], Sc, sprev, None, ALU.subtract),
            reads=[("stash", b, "p2"), ("zc16",)], writes=[K("P")])
        p.emit("scalar", lambda s: s.activation(sx.P[:], sx.P[:], AF.Exp),
               reads=[K("P")], writes=[K("P")])
        p.emit("vector", lambda v, Cc=Cc: v.tensor_mul(sx.CP[:], Cc, sx.P[:]),
               reads=[("stash", b, "p1"), K("P")], writes=[K("CP")])
        p.emit("vector", lambda v, send=send, sprev=sprev: v.tensor_scalar(
            sx.t16[:, 0:1], send, sprev, None, ALU.subtract),
            reads=[("stash", b, "p2"), ("zc16",)], writes=[K("t16")])
        p.emit("scalar", lambda s: s.activation(sx.ptot[:], sx.t16[:, 0:1], AF.Exp),
               reads=[K("t16")], writes=[K("ptot")])

        for n in range(NST):
            ja = n % 2
            jc = (n + 1) % 2
            p.emit("tensor", lambda t, n=n, ja=ja, Sc=Sc: t.matmul(
                sx.psA[:, ja, 0:128], sh.oneh[:, n, :], Sc,
                start=True, stop=True),
                reads=[("oneh",), ("stash", b, "p2")],
                writes=[K("psA", ja)], tag=f"D{n}")
            p.emit("vector", lambda v, ja=ja, n=n: v.tensor_scalar(
                sx.t2[:, ja, :], sx.psA[:, ja, 0:128], sx.ST[:, n:n+1], 0.0,
                ALU.subtract, ALU.min),
                reads=[K("psA", ja), K("ST")], writes=[K("t2", ja)], tag="Dclamp")
            p.emit("scalar", lambda s, ja=ja: s.activation(
                sx.G[:, ja, :], sx.t2[:, ja, :], AF.Exp),
                reads=[K("t2", ja)], writes=[K("G", ja)], tag="Gexp")
            p.emit("tensor", lambda t, n=n, jc=jc, Cc=Cc: t.matmul(
                sx.psA[:, jc, 0:128], sh.oneh[:, n, :], Cc,
                start=True, stop=True),
                reads=[("oneh",), ("stash", b, "p1")],
                writes=[K("psA", jc)], tag=f"C{n}")
            p.emit("vector", lambda v, ja=ja, jc=jc: v.tensor_mul(
                sx.t2[:, ja, :], sx.G[:, ja, :], sx.psA[:, jc, 0:128]),
                reads=[K("G", ja), K("psA", jc)], writes=[K("t2", ja)], tag="GC")
            par = n % 2
            if n < 2:
                p.emit("vector", lambda v, ja=ja, n=n, par=par: v.tensor_scalar_mul(
                    sx.kacc[:, par, :], sx.t2[:, ja, :], sx.dBT[:, n:n+1]),
                    reads=[K("t2", ja), K("dBT")], writes=[K("kacc", par)], tag="K0")
            else:
                p.emit("vector", lambda v, ja=ja, n=n, par=par: v.scalar_tensor_tensor(
                    sx.kacc[:, par, :], sx.t2[:, ja, :], sx.dBT[:, n:n+1],
                    sx.kacc[:, par, :], ALU.mult, ALU.add),
                    reads=[K("t2", ja), K("dBT"), K("kacc", par)],
                    writes=[K("kacc", par)], tag="Kn")
        p.emit("vector", lambda v: v.scalar_tensor_tensor(
            sx.kacc[:, 0, :], sx.kacc[:, 1, :], 1.0, sx.kacc[:, 0, :],
            ALU.mult, ALU.add),
            reads=[K("kacc", 0), K("kacc", 1)], writes=[K("kacc", 0)], tag="Ksum")
        p.emit("vector", lambda v: v.tensor_mul(sx.kbf[:], sx.kacc[:, 0, :],
                                                sh.kmask[:]),
               reads=[K("kacc", 0), ("kmask",)], writes=[K("kbf")], tag="Kmask")

    def rest_emit(self, c):
        p, sh, b, K = self.p, self.sh, self.b, self.K
        sx = self
        cb = c % 2
        t0 = b * SG
        trange = slice(t0 + c*128, t0 + (c+1)*128)
        for ps4 in range(4):
            for j in range(4):
                chc = ps4 * 4 + j
                def mky(t, chc=chc, j=j, cb=cb):
                    t.matmul(sx.psY[:, j, :], sx.utc[:, cb, chc, :], sx.kbf[:],
                             start=True, stop=False)
                    return t.matmul(sx.psY[:, j, :],
                                    sx.hstbf[:, chc*128:(chc+1)*128],
                                    sx.CP[:], start=False, stop=True)
                p.emit("tensor", mky,
                       reads=[K("utc", cb), K("kbf"), K("hstbf"), K("CP")],
                       writes=[K("psY")], tag=f"y{chc}")
            sl = slice(ps4*4*128, (ps4+1)*4*128)
            p.emit("vector", lambda v, ps4=ps4, sl=sl, cb=cb: v.tensor_add(
                sx.ys[:].rearrange("p c n -> p (c n)")[:, sl],
                sx.psY[:].rearrange("p c n -> p (c n)"),
                sx.uDc[:, cb, :, :].rearrange("p c n -> p (c n)")[:, sl]),
                reads=[K("psY"), K("uDc", cb)], writes=[K("ys")], tag="yadd")
            p.emit("vector", lambda v, sl=sl: v.tensor_mul(
                sx.ys[:].rearrange("p c n -> p (c n)")[:, sl],
                sx.ys[:].rearrange("p c n -> p (c n)")[:, sl],
                sx.szc[:].rearrange("p c n -> p (c n)")[:, sl]),
                reads=[K("ys"), K("szc")], writes=[K("ys")], tag="ymul")
        for m in range(4):
            def mkh(t, m=m, cb=cb):
                return t.matmul(sx.psW[0:16, :], sx.EdBT[:],
                                sx.utc[:, cb, m*4:(m+1)*4, :],
                                start=True, stop=True)
            p.emit("tensor", mkh, reads=[K("EdBT"), K("utc", cb)],
                   writes=[K("psW")], tag=f"h{m}")
            p.emit("vector", lambda v, m=m: v.scalar_tensor_tensor(
                sx.hst[:, m*512:(m+1)*512], sx.hst[:, m*512:(m+1)*512],
                sx.ptot[:], sx.psW[0:16, :], ALU.mult, ALU.add),
                reads=[K("psW"), K("ptot"), K("hst")], writes=[K("hst")],
                tag="hupd")
        p.emit("scalar", lambda s: s.copy(sx.hstbf[:], sx.hst[:]),
               reads=[K("hst")], writes=[K("hstbf")], tag="hbf")
        p.dma("sync", f"lnx{b}", sx.xr[:], self.xsrc[trange, :],
              reads=[(self.xsrc_key, b, c)], writes=[K("xr")], tag="xr2")
        for dh in range(2):
            def mko(t, dh=dh):
                last = None
                for cc2 in range(16):
                    last = t.matmul(sx.psW[:], sx.ys[:, cc2, :],
                                    sh.wout[:, cc2, dh*512:(dh+1)*512],
                                    start=(cc2 == 0), stop=(cc2 == 15))
                return last
            p.emit("tensor", mko, reads=[K("ys"), ("wout",)],
                   writes=[K("psW")], tag=f"wo{dh}")
            p.emit("vector", lambda v, dh=dh: v.tensor_add(
                sx.x3[:, dh*512:(dh+1)*512], sx.psW[:],
                sx.xr[:, dh*512:(dh+1)*512]),
                reads=[K("psW"), K("xr")], writes=[K("x3")], tag="res3")
        p.dma("sync", f"xo{b}", self.xdst[trange, :], sx.x3[:],
              reads=[K("x3")], writes=[(self.xdst_key, b, c)], tag="x3o")


def emit_mamba_b2(p, lx, mx, b, xsrc, xsrc_key, stash):
    utc_d, uD_d, sz_d = stash["utc"], stash["uD"], stash["sz"]
    emit_ln_sg(p, lx, mx.cc, xsrc, b, xsrc_key)
    t0 = b * SG
    STQ = ("gpsimd", "gpsimd", "gpsimd")
    stoc = [0]

    def sto_dma(out, in_, reads, writes, tag):
        eng = STQ[stoc[0] % 3]
        stoc[0] += 1
        p.dma(eng, "sto_" + eng, out, in_, reads=reads, writes=writes, tag=tag)

    for cc in range(32):
        xib = cc % 2
        ub = 0
        pi, f = cc // 8, cc % 8
        if f == 0:
            if pi == 0:
                p.dma("gpsimd", "wp", mx.win[:, 0, :], mx.win_d[pi, :, :],
                      writes=[("win", 0)], tag="winp")
            if pi + 1 < 4:
                p.dma("gpsimd", "wp", mx.win[:, (pi + 1) % 2, :],
                      mx.win_d[pi + 1, :, :],
                      writes=[("win", (pi + 1) % 2)], tag="winp")
        for g in range(4):
            pp = g % 2
            def mkxz(t, pi=pi, f=f, g=g, pp=pp):
                last = None
                for kc in range(8):
                    last = t.matmul(
                        mx.ps_xz[:, pp, :],
                        mx.win[:, pi % 2, f*1024 + kc*128: f*1024 + (kc+1)*128],
                        lx.xlnT[:, kc, g*512:(g+1)*512],
                        start=(kc == 0), stop=(kc == 7))
                return last
            p.emit("tensor", mkxz, reads=[("win", pi % 2), ("xlnT",)],
                   writes=[("psxz", pp)], tag=f"xz{cc}.{g}")
            if cc < 16:
                p.emit("vector", lambda v, g=g, pp=pp, xib=xib: v.tensor_copy(
                    mx.xi[:, xib, 3 + g*512: 3 + (g+1)*512], mx.ps_xz[:, pp, :]),
                    reads=[("psxz", pp)], writes=[("xi", xib)], tag="xiev")
            else:
                zb = g % 2
                p.emit("scalar", lambda s, g=g, pp=pp, zb=zb: s.activation(
                    mx.sz[:, zb, :], mx.ps_xz[:, pp, :], AF.Silu),
                    reads=[("psxz", pp)], writes=[("szrow", zb)], tag="siluz")
                sto_dma(sz_d[(cc-16)*128:(cc-16+1)*128,
                             t0 + g*512: t0 + (g+1)*512],
                        mx.sz[:, zb, :],
                        reads=[("szrow", zb)], writes=[("szst", b, cc, g)],
                        tag="szo")
        if cc < 16:
            u_ = mx.u[:, ub, :]
            uD_ = mx.uD[:]
            ubf_ = mx.ubf[:, ub, :]
            p.emit("vector", lambda v, cc=cc, xib=xib, u_=u_: v.tensor_scalar_mul(
                u_, mx.xi[:, xib, 0:2048], mx.cw[:, cc, 0:1]),
                reads=[("xi", xib), ("cw",)], writes=[("u", ub)], tag="cv0")
            for j in (1, 2):
                p.emit("vector", lambda v, j=j, cc=cc, xib=xib, u_=u_:
                       v.scalar_tensor_tensor(
                    u_, mx.xi[:, xib, j:j+2048], mx.cw[:, cc, j:j+1],
                    u_, ALU.mult, ALU.add),
                    reads=[("xi", xib), ("cw",), ("u", ub)], writes=[("u", ub)],
                    tag=f"cv{j}")
            p.emit("vector", lambda v, cc=cc, xib=xib, u_=u_:
                   v.scalar_tensor_tensor(
                u_, mx.xi[:, xib, 3:3+2048], mx.cw[:, cc, 3:4],
                u_, ALU.mult, ALU.add),
                reads=[("xi", xib), ("cw",), ("u", ub)], writes=[("u", ub)],
                tag="cv3")
            p.emit("scalar", lambda s, cc=cc, u_=u_: s.activation(
                u_, u_, AF.Silu, bias=mx.cb[:, cc:cc+1]),
                reads=[("u", ub), ("cb",)], writes=[("u", ub)], tag="silu_u")
            p.emit("vector", lambda v, cc=cc, u_=u_, uD_=uD_: v.tensor_scalar_mul(
                uD_, u_, mx.dd[:, cc:cc+1]),
                reads=[("u", ub), ("dd",)], writes=[("uDr",)], tag="uD")
            p.emit("scalar", lambda s, u_=u_, ubf_=ubf_: s.copy(ubf_, u_),
                   reads=[("u", ub)], writes=[("ubf", ub)], tag="ubf")
            p.dma("gpsimd", "sto_gpsimd", uD_d[cc*128:(cc+1)*128, t0:t0+SG],
                  uD_, reads=[("uDr",)], writes=[("uDst", b, cc)], tag="uDo")
            def mkbc(t, cc=cc, ubf_=ubf_):
                last = None
                for g in range(4):
                    last = t.matmul(mx.ps_bc[:, g*512:(g+1)*512],
                                    mx.wbc[:, cc, :],
                                    ubf_[:, g*512:(g+1)*512],
                                    start=(cc == 0), stop=(cc == 15))
                return last
            p.emit("tensor", mkbc, reads=[("wbc",), ("ubf", ub)],
                   writes=[("psbc",)], tag=f"bc{cc}")
            for c in range(16):
                tb = c % 2
                p.emit("tensor", lambda t, c=c, tb=tb, ubf_=ubf_: t.transpose(
                    mx.ps_tp[:, tb, 0:128], ubf_[:, c*128:(c+1)*128], p.idbf[:]),
                    reads=[("ubf", ub), ("idbf",)], writes=[("pstp", tb)],
                    tag="utp")
                p.emit("scalar", lambda s, c=c, tb=tb: s.copy(
                    mx.utc[:, ub, c, :], mx.ps_tp[:, tb, 0:128]),
                    reads=[("pstp", tb)], writes=[("utcs", ub)], tag="utce")
            sto_dma(utc_d[cc, t0:t0+SG, :].rearrange("(c p) n -> p c n", p=128),
                    mx.utc[:, ub, :, :],
                    reads=[("utcs", ub)], writes=[("utcst", b, cc)], tag="utco")

    Bm, Cm, Ssc = stash["Bm"], stash["Cm"], stash["S"]
    dB = stash["dB"]
    p.emit("vector", lambda v: v.tensor_copy(Bm[:], mx.ps_bc[0:16, :]),
           reads=[("psbc",)], writes=[("stash", b, "p1")])
    p.emit("vector", lambda v: v.tensor_copy(Cm[:], mx.ps_bc[32:48, :]),
           reads=[("psbc",)], writes=[("stash", b, "p1")])
    p.emit("scalar", lambda s: s.activation(mx.dpre[:], mx.ps_bc[64:80, :],
                                            AF.Identity, bias=p.bdt[:]),
           reads=[("psbc",), ("bdt",)], writes=[("dpre",)])
    p.emit("scalar", lambda s: s.activation(mx.da[:], mx.dpre[:], AF.Abs),
           reads=[("dpre",)], writes=[("da",)])
    p.emit("scalar", lambda s: s.activation(mx.da[:], mx.da[:], AF.Exp, scale=-1.0),
           reads=[("da",)], writes=[("da",)])
    p.emit("scalar", lambda s: s.activation(mx.da[:], mx.da[:], AF.Ln, bias=1.0),
           reads=[("da",)], writes=[("da",)])
    p.emit("vector", lambda v: v.tensor_relu(mx.db[:], mx.dpre[:]),
           reads=[("dpre",)], writes=[("db",), ("dpre",)])
    p.emit("vector", lambda v: v.tensor_add(mx.db[:], mx.db[:], mx.da[:]),
           reads=[("db",), ("da",)], writes=[("db",)])
    p.emit("vector", lambda v: v.tensor_mul(dB[:], mx.db[:], Bm[:]),
           reads=[("db",), ("stash", b, "p1")], writes=[("stash", b, "p2")])
    p.emit("vector", lambda v: v.tensor_scalar_mul(mx.da[:], mx.db[:],
                                                   p.acol[:]),
           reads=[("db",), ("acol",)], writes=[("da",)])
    p.emit("vector", lambda v: v.tensor_tensor_scan(
        Ssc[:], mx.da[:], mx.da[:], 0.0, ALU.add, ALU.bypass),
        reads=[("da",)], writes=[("stash", b, "p2")], tag="scan")


def _bar_and_flush(p):
    for e in ENGINES:
        p.wait_all_current(e)
    p.build_block()


def build_program():
    nc = bass.Bass("TRN2", target_bir_lowering=False)
    dr = lambda n, s, dt, k: nc.dram_tensor(n, s, dt, kind=k).ap()
    x_d = dr("x", [NTOK, D], F32, "ExternalInput")
    wq_d = dr("wq", [128, 8192], F32, "ExternalInput")
    wk_d = dr("wk", [128, 8192], F32, "ExternalInput")
    wv_d = dr("wv", [128, 8, 1024], F32, "ExternalInput")
    wo_d = dr("wo", [128, 8, 1024], F32, "ExternalInput")
    gma_d = dr("gma", [128, NH, 128], F32, "ExternalInput")
    gmaw_d = dr("gmaw", [128, NH, 512], F32, "ExternalInput")
    gnw_d = dr("gnw", [64, 16], F32, "ExternalInput")
    gnb_d = dr("gnb", [64, 16], F32, "ExternalInput")
    wffn_d = dr("wffn", [NE, NPIECE, 128, 16384], F32, "ExternalInput")
    cent_d = dr("centT", [D, 8], F32, "ExternalInput")
    rb_d = dr("rbias", [128, 8], F32, "ExternalInput")
    win_d = dr("win", [4, 128, 8192], F32, "ExternalInput")
    wbc_d = dr("wbc", [128, 16, 96], F32, "ExternalInput")
    cw_d = dr("cw", [128, 16, 4], F32, "ExternalInput")
    cb_d = dr("cb", [128, 16], F32, "ExternalInput")
    dd_d = dr("dd", [128, 16], F32, "ExternalInput")
    bdt_d = dr("bdt", [16, 1], F32, "ExternalInput")
    alog_d = dr("alog", [16, 1], F32, "ExternalInput")
    wout_d = dr("wout", [128, 16, 1024], F32, "ExternalInput")
    oneh_d = dr("oneh", [16, 16, 128], F32, "ExternalInput")
    kmask_d = dr("kmask", [128, 128], F32, "ExternalInput")
    idf_d = dr("idf", [128, 128], F32, "ExternalInput")
    out_d = dr("out", [NTOK, D], F32, "ExternalOutput")

    x1_d = dr("x1s", [NTOK, D], F32, "Internal")
    x2_d = dr("x2s", [NTOK, D], F32, "Internal")
    x3_d = dr("x3s", [NTOK, D], F32, "Internal")
    qT_d = dr("qTs", [D, NTOK], BF16, "Internal")
    kT_d = dr("kTs", [D, NTOK], BF16, "Internal")
    v_d = dr("vSs", [NTOK, D], BF16, "Internal")
    utc_d = dr("utcS", [16, NTOK, 128], BF16, "Internal")
    uD_d = dr("uDS", [DIN, NTOK], BF16, "Internal")
    sz_d = dr("szS", [DIN, NTOK], BF16, "Internal")

    gam_pows = [1.0 - 2.0 ** (-5.0 - i) for i in range(NH)]

    es = contextlib.ExitStack()
    with es:
        p = Prog(nc, es)
        cc = ConstCols(p, es, {"eps": EPS, "gneps": GN_EPS})
        p.idbf = sbuf(nc, es, "idbf", [128, 128], BF16)[:]
        p.bdt = sbuf(nc, es, "bdtT", [16, 1], F32)[:]
        p.acol = sbuf(nc, es, "acol", [16, 1], F32)[:]
        atmp = sbuf(nc, es, "atmp", [16, 1], F32)[:]
        p.dma("gpsimd", "c", p.idbf, idf_d, writes=[("idbf",)])
        p.dma("gpsimd", "c", p.bdt, bdt_d, writes=[("bdt",)])
        p.dma("gpsimd", "c", atmp, alog_d, writes=[("alog",)])
        p.emit("scalar", lambda s: s.activation(atmp, atmp, AF.Exp),
               reads=[("alog",)], writes=[("alog",)])
        p.emit("scalar", lambda s: s.mul(p.acol, atmp, -1.0),
               reads=[("alog",)], writes=[("acol",)])

        with contextlib.ExitStack() as bs:
            lx = LnCtx(p, bs, "a1")
            ax = AttnProjCtx(p, bs)
            ax.qT, ax.kT, ax.vS = qT_d, kT_d, v_d
            for j in range(4):
                sl = slice(j * 2048, (j + 1) * 2048)
                p.dma("gpsimd", "c", ax.wq[:, sl], wq_d[:, sl],
                      writes=[("wh", 0, j)])
            for j in range(4):
                sl = slice(j * 2048, (j + 1) * 2048)
                p.dma("gpsimd", "c", ax.wk[:, sl], wk_d[:, sl],
                      writes=[("wh", 1, j)])
            p.dma("gpsimd", "c", ax.wv[:], wv_d, writes=[("wh", 2, 0)])
            for sg in range(2):
                emit_ln_sg(p, lx, cc, x_d, sg, "xin")
                emit_attn_a_proj(p, lx, ax, sg)
            _bar_and_flush(p)

        with contextlib.ExitStack() as bs:
            rx = RetCtx(p, bs)
            rx.cc = cc
            p.dma("gpsimd", "c", rx.gma[:], gma_d, writes=[("gma",)])
            p.dma("gpsimd", "c", rx.w0[:], gmaw_d, writes=[("w0",)])
            p.dma("gpsimd", "c", rx.gnw[:], gnw_d, writes=[("gnw",)])
            p.dma("gpsimd", "c", rx.gnb[:], gnb_d, writes=[("gnb",)])
            p.emit("vector", lambda v: v.memset(rx.ones64[:], 1.0),
                   writes=[("ones64",)])
            p.emit("vector", lambda v: v.memset(rx.onesr[:], 1.0),
                   writes=[("onesr",)])
            p.dma("gpsimd", "c", rx.wo[:], wo_d, writes=[("wo",)])
            for b in range(2):
                emit_retention_batch(p, rx, b, qT_d, kT_d, v_d, x_d, x1_d,
                                     "xin", "x1", gam_pows)
            _bar_and_flush(p)

        with contextlib.ExitStack() as bs:
            fx = FfnCtx(p, bs)
            fx.cc = cc
            fx.wffn_ap = wffn_d
            fx.load_consts(p, cent_d, rb_d)
            emit_ffn_call(p, fx, x1_d, x2_d, 0, "x1", "x2")
            _bar_and_flush(p)

        with contextlib.ExitStack() as mams:
            packs = []
            for b in range(2):
                packs.append({
                    "utc": utc_d, "uD": uD_d, "sz": sz_d,
                    "Cm": sbuf(nc, mams, f"stCm_{b}", [16, 2048], F32)[:],
                    "S": sbuf(nc, mams, f"stS_{b}", [16, 2048], F32)[:],
                    "dB": sbuf(nc, mams, f"stdB_{b}", [16, 2048], F32)[:],
                })
            with contextlib.ExitStack() as bs:
                mx = MbCtx(p, bs)
                lx = LnCtx(p, bs, "m1", stream=True, tp=mx.ps_tp, tpkey="pstp")
                mx.cc = cc
                mx.win_d = win_d
                bm_scr = sbuf(nc, bs, "stBm", [16, 2048], F32)[:]
                for pk in packs:
                    pk["Bm"] = bm_scr
                p.dma("gpsimd", "c", mx.wbc[:], wbc_d, writes=[("wbc",)])
                p.dma("gpsimd", "c", mx.cw[:], cw_d, writes=[("cw",)])
                p.dma("gpsimd", "c", mx.cb[:], cb_d, writes=[("cb",)])
                p.dma("gpsimd", "c", mx.dd[:], dd_d, writes=[("dd",)])
                p.emit("vector", lambda v: v.memset(mx.xi[:, 0, 0:3], 0.0),
                       writes=[("xi", 0)])
                p.emit("vector", lambda v: v.memset(mx.xi[:, 1, 0:3], 0.0),
                       writes=[("xi", 1)])
                for b in range(2):
                    emit_mamba_b2(p, lx, mx, b, x2_d, "x2", packs[b])
                    _bar_and_flush(p)
            with contextlib.ExitStack() as bs:
                sh = ScanShared(p, bs)
                p.dma("gpsimd", "c", sh.wout[:], wout_d, writes=[("wout",)])
                p.dma("gpsimd", "c", sh.oneh[:], oneh_d, writes=[("oneh",)])
                p.dma("gpsimd", "c", sh.kmask[:], kmask_d, writes=[("kmask",)])
                p.dma("gpsimd", "c", sh.idf[:], idf_d, writes=[("idf",)])
                p.emit("vector", lambda v: v.memset(sh.zc16[:], 0.0),
                       writes=[("zc16",)])
                sbs = [ScanBatch(p, bs, sh, b, packs[b], x2_d, x3_d, "x2", "x3")
                       for b in range(2)]
                for b in range(2):
                    sbs[b].init_emit()
                for c in range(16):
                    for b in range(2):
                        sbs[b].prep_emit(c)
                    for b in range(2):
                        sbs[b].rest_emit(c)
                _bar_and_flush(p)

        with contextlib.ExitStack() as bs:
            fx = FfnCtx(p, bs)
            fx.cc = cc
            fx.wffn_ap = wffn_d
            fx.load_consts(p, cent_d, rb_d)
            emit_ffn_call(p, fx, x3_d, out_d, 1, "x3", "xo")
            _bar_and_flush(p)
    return nc


def _host_pack(inp):
    Wq = inp["L0_attn_a_W_q"]; Wk = inp["L0_attn_a_W_k"]
    Wv = inp["L0_attn_a_W_v"]; Wo = inp["L0_attn_a_W_o"]
    oneh = np.zeros((16, 16, 128), np.float32)
    for n in range(16):
        oneh[n, n, :] = 1.0
    kmask = (np.arange(128)[None, :] >= np.arange(128)[:, None]).astype(np.float32)
    W_x = inp["L0_attn_b_W_x"]; W_dt = inp["L0_attn_b_W_dt"]
    wbc = np.concatenate([
        np.pad(np.ascontiguousarray(W_x.T[:, 0:16]), ((0, 0), (0, 16))),
        np.pad(np.ascontiguousarray(W_x.T[:, 16:32]), ((0, 0), (0, 16))),
        np.pad(np.ascontiguousarray(W_dt.T), ((0, 0), (0, 16)))], axis=1)
    shared = {
        "wq": pack_lhsT_blob(Wq),
        "wk": pack_lhsT_blob(Wk),
        "wv": pack_rhs_blob(Wv.T),
        "wo": pack_rhs_blob(Wo.T),
        "gma": np.ascontiguousarray(make_gamma_tiles()[:, 1].transpose(1, 0, 2)),
        "gmaw": make_gamma_wide(),
        "gnw": np.ascontiguousarray(inp["L0_attn_a_gn_weight"].reshape(16, 64).T),
        "gnb": np.ascontiguousarray(inp["L0_attn_a_gn_bias"].reshape(16, 64).T),
        "wffn": pack_ffn_weights(inp["L0_ffn_Wfc_r"], inp["L0_ffn_Wproj_r"],
                                 inp["L0_ffn_Wfc_sh"], inp["L0_ffn_Wproj_sh"]),
        "centT": np.ascontiguousarray(inp["L0_ffn_centroids"].T),
        "rbias": np.ascontiguousarray(
            np.broadcast_to(inp["L0_ffn_routing_bias"], (128, 8))),
        "win": pack_fc_blob(inp["L0_attn_b_W_in"], 4),
        "wbc": np.ascontiguousarray(wbc.reshape(16, 128, 96).transpose(1, 0, 2)),
        "cw": np.ascontiguousarray(
            inp["L0_attn_b_conv_w"][:, 0, :].reshape(16, 128, 4).transpose(1, 0, 2)),
        "cb": np.ascontiguousarray(inp["L0_attn_b_conv_b"].reshape(16, 128).T),
        "dd": np.ascontiguousarray(inp["L0_attn_b_D"].reshape(16, 128).T),
        "bdt": np.ascontiguousarray(inp["L0_attn_b_b_dt"].reshape(16, 1)),
        "alog": np.ascontiguousarray(inp["L0_attn_b_A_log"].reshape(16, 1)),
        "wout": pack_rhs_blob(inp["L0_attn_b_W_out"].T),
        "oneh": oneh, "kmask": kmask,
        "idf": np.eye(128, dtype=np.float32),
    }
    shared = {k: np.ascontiguousarray(v, dtype=np.float32)
              for k, v in shared.items()}
    return shared


def kernel(**inputs):
    x_full = np.asarray(inputs["L0_attn_a_x"], dtype=np.float32)
    shared = _host_pack(inputs)
    nc = build_program()
    in_maps = []
    for c in range(8):
        m = dict(shared)
        m["x"] = np.ascontiguousarray(
            x_full[2*c:2*c+2].reshape(NTOK, D))
        in_maps.append(m)
    res = run_bass_kernel_spmd(nc, in_maps, core_ids=list(range(8)))
    out = np.empty((16, 2048, 1024), np.float32)
    for c in range(8):
        out[2*c:2*c+2] = res.results[c]["out"].reshape(2, 2048, 1024)
    return out
